# version 35
# baseline (speedup 1.0000x reference)
import numpy as np
import ml_dtypes
from contextlib import ExitStack

import concourse.bass as bass
import concourse.mybir as mybir
from concourse.bass_utils import run_bass_kernel_spmd

F32 = mybir.dt.float32
BF16 = mybir.dt.bfloat16
I32 = mybir.dt.int32
ALU = mybir.AluOpType
AF = mybir.ActivationFunctionType
AX = mybir.AxisListType
BFNP = ml_dtypes.bfloat16

DM = 1024
NH = 16
DH = 64
IFF = 2816
NKC = DM // 128
NIB = IFF // 128
NEU = 3


class Sched:

    ENGS = ("tensor", "vector", "scalar", "gpsimd", "sync")
    STREAMS = ("dma_s", "dma_g", "cc1", "cc2", "cc3",
               "lda0", "lda1", "spk0", "spk1",
               "ldk0", "ldk1", "spst", "sppr", "sprt",
               "ld20", "ld21", "hst0", "hst1", "spT0", "spT1", "wW0", "wW1",
               "ldd0", "ldd1", "eht0", "eht1", "ext0", "ext1",
               "wq0", "wq1",
               "sto0", "sto1", "dbgs")

    def __init__(self, nc, stack):
        self.nc = nc
        self.sem = {}
        self.cnt = {}
        for e in self.ENGS + self.STREAMS:
            self.sem[e] = stack.enter_context(nc.semaphore(f"s_{e}"))
            self.cnt[e] = 0
        self.lists = {e: [] for e in self.ENGS}
        self.sat = {e: {} for e in self.ENGS}

    def _waitlist(self, eng, waits):
        wl = []
        for w in waits:
            if w is None:
                continue
            s, v = w
            if self.sat[eng].get(s, 0) >= v:
                continue
            wl.append((s, v))
            self.sat[eng][s] = v
        return tuple(wl)

    def g(self, eng, fn, waits=(), key=None, inc=1):
        key = key or eng
        self.cnt[key] += inc
        tgt = self.cnt[key]
        wl = self._waitlist(eng, waits)

        def closure(e, fn=fn, wl=wl, key=key, tgt=tgt, inc=inc):
            for s, v in wl:
                e.wait_ge(self.sem[s], v)
            last = fn(e)
            last.then_inc(self.sem[key], inc)
        self.lists[eng].append(closure)
        return (key, tgt)

    def gdma(self, eng, fns, waits=(), key=None):
        key = key or ("dma_s" if eng == "sync" else "dma_g")
        prev = (key, self.cnt[key]) if self.cnt[key] > 0 else None
        self.cnt[key] += 16 * len(fns)
        tgt = self.cnt[key]
        wl = self._waitlist(eng, tuple(waits) + (prev,))

        def closure(e, fns=tuple(fns), wl=wl, key=key):
            for s, v in wl:
                e.wait_ge(self.sem[s], v)
            for f in fns:
                f(e).then_inc(self.sem[key], 16)
        self.lists[eng].append(closure)
        return (key, tgt)

    def barrier(self):
        return tuple((e, self.cnt[e]) for e in self.ENGS + self.STREAMS
                     if self.cnt[e] > 0)

    def finalize(self, block):
        for name in self.ENGS:
            lst = self.lists[name]

            def body(e, lst=lst):
                for c in lst:
                    c(e)
            getattr(block, name)(body)


def build_nc(T, n_cores=8, debug_outs=False, stop_after=None):
    NT = T // 128
    nc = bass.Bass("TRN2", target_bir_lowering=False, debug=False)

    x_in = nc.dram_tensor("x", [T, DM], F32, kind="ExternalInput").ap()
    W_in = nc.dram_tensor("W", [DM, DM], F32, kind="ExternalInput").ap()
    wg_in = nc.dram_tensor("wg", [IFF, DM], F32, kind="ExternalInput").ap()
    wu_in = nc.dram_tensor("wu", [IFF, DM], F32, kind="ExternalInput").ap()
    wd_in = nc.dram_tensor("wd", [DM, IFF], F32, kind="ExternalInput").ap()
    lng_in = nc.dram_tensor("ln_g", [1, DM], F32, kind="ExternalInput").ap()
    lnb_in = nc.dram_tensor("ln_b", [1, DM], F32, kind="ExternalInput").ap()
    eta_in = nc.dram_tensor("eta", [1, 1], F32, kind="ExternalInput").ap()
    lml_in = nc.dram_tensor("lam_logit", [1, 1], F32, kind="ExternalInput").ap()
    cx_in = nc.dram_tensor("clip_x", [1, 1], F32, kind="ExternalInput").ap()
    cd_in = nc.dram_tensor("clip_down", [1, 1], F32, kind="ExternalInput").ap()
    sel_in = nc.dram_tensor("sel", [64, n_cores], F32, kind="ExternalInput").ap()
    ind_in = nc.dram_tensor("ind_bits", [7, 128, 128], F32, kind="ExternalInput").ap()
    strict_in = nc.dram_tensor("strictT", [128, 128], F32, kind="ExternalInput").ap()
    incl_in = nc.dram_tensor("inclT", [128, 128], F32, kind="ExternalInput").ap()
    i128b_in = nc.dram_tensor("i128_bf16", [128, 128], BF16, kind="ExternalInput").ap()
    i128f_in = nc.dram_tensor("i128_f32", [128, 128], F32, kind="ExternalInput").ap()
    i64f_in = nc.dram_tensor("i64_f32", [64, 64], F32, kind="ExternalInput").ap()
    ones_in = nc.dram_tensor("ones_row", [1, 128], F32, kind="ExternalInput").ap()

    out_t = nc.dram_tensor("out", [T, DM], F32, kind="ExternalOutput").ap()

    xactT_sp = nc.dram_tensor("xactT_sp", [NT, 128, NKC, 128], BF16).ap()
    xacttok_sp = nc.dram_tensor("xacttok_sp", [NT, 128, DM], BF16).ap()
    static_sp = nc.dram_tensor("static_sp", [NT, 128, DM], BF16).ap()
    pr_sp = nc.dram_tensor("pr_sp", [NT, 128, NH * DH], BF16).ap()
    rt_sp = nc.dram_tensor("rt_sp", [NT, 64, NH * 128], BF16).ap()
    h_sp = nc.dram_tensor("h_sp", [IFF, T], BF16).ap()
    dp_sp = nc.dram_tensor("dp_sp", [520], F32).ap()
    row_sp = nc.dram_tensor("row_sp", [1024], F32).ap()
    wlev_sp = nc.dram_tensor("wlev_sp", [2 * NIB, 128, DM], BF16).ap()
    cb1_i = nc.dram_tensor("cb1_i", [16], F32).ap()
    cb1_o = nc.dram_tensor("cb1_o", [16], F32).ap()
    cb2_i = nc.dram_tensor("cb2_i", [16], F32).ap()
    cb2_o = nc.dram_tensor("cb2_o", [16], F32).ap()
    cb3_i = nc.dram_tensor("cb3_i", [64 * NH * DH], F32).ap()
    cb3_o = nc.dram_tensor("cb3_o", [n_cores, 64 * NH * DH], F32).ap()

    dbg = {}
    if debug_outs:
        for nm, shp in [("d_xn", [128, DM]), ("d_static", [128, DM]),
                        ("d_gmax", [1, 1]), ("d_hmax", [1, 1]),
                        ("d_xlev", [128, DM]), ("d_y", [128, DM]),
                        ("d_sstart", [64, NH * DH])]:
            dbg[nm] = nc.dram_tensor(nm, shp, F32, kind="ExternalOutput").ap()

    stack = ExitStack()
    with stack:
        sc = Sched(nc, stack)
        G = sc.g
        GD = sc.gdma
        dp_stack = ExitStack()

        def alloc(stk, name, shape, dtype):
            return stk.enter_context(nc.sbuf_tensor(name, shape, dtype))

        sb_tiny = alloc(stack, "tiny", [1, 64], F32)
        sb_cols = alloc(stack, "cols", [128, 24], F32)
        sb_stats = alloc(stack, "stats", [128, max(2 * NT, 16)], F32)
        sb_mD = alloc(stack, "maskD", [128, NH * 128], BF16)
        sb_mA = alloc(stack, "maskA", [128, NH * 128], BF16)
        sb_i128b = alloc(stack, "i128b", [128, 128], BF16)
        sb_i128f = alloc(stack, "i128f", [128, 128], F32)
        sb_ks = alloc(stack, "kstile", [128, NH * DH], F32)
        sb_ones = alloc(stack, "onesrow", [1, 128], F32)
        sb_onesb = alloc(stack, "onesrowb", [1, 128], BF16)
        sb_scr = alloc(stack, "scr", [128, 1024], F32)
        sb_scr2 = alloc(stack, "scr2", [128, 1024], F32)
        sb_wscale = alloc(stack, "wscale", [128, 2 * NIB + 8], F32)
        sb_S0 = alloc(stack, "S0_tile", [64, NH * DH], BF16)
        sb_sel = alloc(stack, "selrep", [64, n_cores], F32)
        sb_dp = dp_stack.enter_context(nc.sbuf_tensor("dp", [1, 520], F32))

        ps_a = stack.enter_context(nc.psum_tensor("PA", [128, 2048], F32))
        ps_b = stack.enter_context(nc.psum_tensor("PB", [128, 2048], F32))

        COL = {"beta": 0, "nbeta": 1, "lamC": 2, "scalex": 3, "invx": 4,
               "scaleh": 5, "invh": 6, "lam_t": 7, "lam_ct": 8, "blam_t": 9,
               "q0": 10, "c1285": 17}
        def col(name, off=0):
            c = COL[name] + off
            return sb_cols[:, c:c + 1]

        TI = {"lam": 0, "one_m": 1, "beta": 2, "sigcx": 3, "sigcd": 4,
              "gmax": 5, "hmax": 6, "scalex": 7, "invx": 8, "scaleh": 9,
              "invh": 10, "t1": 11, "t2": 12, "pad": 16}
        def tiny(name, n=1):
            i = TI[name]
            return sb_tiny[0:1, i:i + n]

        def v3(tensor, offset, pstride, mid_n, mid_stride, inner_n, np_=128):
            return bass.AP(tensor=tensor, offset=offset,
                           ap=[[pstride, np_], [mid_stride, mid_n], [1, inner_n]])

        t = [None]
        def S(eng, fn, extra=(), key=None, inc=1):
            t[0] = G(eng, fn, waits=(t[0],) + tuple(extra), key=key, inc=inc)
            return t[0]
        def SD(eng, fns, extra=(), key=None):
            t[0] = GD(eng, fns, waits=(t[0],) + tuple(extra), key=key)
            return t[0]

        S("vector", lambda e: e.memset(sb_cols[:, COL["c1285"]:COL["c1285"] + 1], 128.5))
        SD("sync", [lambda e: e.dma_start(out=sb_i128b[:, :], in_=i128b_in),
                    lambda e: e.dma_start(out=sb_i128f[:, :], in_=i128f_in),
                    lambda e: e.dma_start(out=sb_ones[:, :], in_=ones_in),
                    lambda e: e.dma_start(out=sb_sel[:, :], in_=sel_in),
                    lambda e: e.dma_start(out=tiny("lam"), in_=lml_in),
                    lambda e: e.dma_start(out=tiny("one_m"), in_=eta_in),
                    lambda e: e.dma_start(out=tiny("sigcx"), in_=cx_in),
                    lambda e: e.dma_start(out=tiny("sigcd"), in_=cd_in)])
        S("scalar", lambda e: e.activation(tiny("lam"), tiny("lam"), AF.Sigmoid))
        S("vector", lambda e: e.tensor_scalar(tiny("t1"), tiny("lam"), -1.0, 1.0, ALU.mult, ALU.add))
        S("vector", lambda e: e.tensor_tensor(tiny("t2"), tiny("t1"), tiny("one_m"), ALU.mult))
        S("vector", lambda e: e.tensor_scalar_mul(tiny("beta"), tiny("t2"), 1.0 / DH))
        S("scalar", lambda e: e.activation(tiny("sigcx"), tiny("sigcx"), AF.Sigmoid))
        S("scalar", lambda e: e.activation(tiny("sigcd"), tiny("sigcd"), AF.Sigmoid))
        S("vector", lambda e: e.memset(sb_dp[0:1, :], 0.0))
        S("vector", lambda e: e.memset(sb_dp[0:1, 0:1], 1.0))
        S("vector", lambda e: e.tensor_copy(sb_dp[0:1, 1:2], tiny("lam")))
        n_have = 2
        while n_have < 257:
            n_new = min(n_have - 1, 257 - n_have)
            def dbl(e, n_have=n_have, n_new=n_new):
                return e.tensor_scalar(sb_dp[0:1, n_have:n_have + n_new],
                                       sb_dp[0:1, 1:1 + n_new],
                                       sb_dp[0:1, n_have - 1:n_have], None, ALU.mult)
            S("vector", dbl)
            n_have += n_new
        S("vector", lambda e: e.tensor_copy(
            sb_dp[0:1, 260:260 + 257],
            bass.AP(tensor=sb_dp, offset=256, ap=[[520, 1], [-1, 257]])))
        SD("gpsimd", [lambda e: e.dma_start(out=dp_sp[0:520], in_=sb_dp[0:1, 0:520])])
        SD("sync", [lambda e: e.dma_start(
                        out=col("lam_t"),
                        in_=bass.AP(tensor=dp_sp.tensor, offset=1, ap=[[1, 128], [1, 1]])),
                    lambda e: e.dma_start(
                        out=col("lam_ct"),
                        in_=bass.AP(tensor=dp_sp.tensor, offset=260 + 129, ap=[[1, 128], [1, 1]]))])

        def emit_rep(src_ap, dst_col, scale_mult=None, add_const=None):
            S("tensor", lambda e: e.matmul(ps_a[:, 0:1], sb_ones[0:1, :], src_ap,
                                           start=True, stop=True))
            if scale_mult is None and add_const is None:
                S("vector", lambda e: e.tensor_copy(dst_col, ps_a[:, 0:1]))
            else:
                sm = 1.0 if scale_mult is None else scale_mult
                ac = 0.0 if add_const is None else add_const
                S("vector", lambda e: e.tensor_scalar(dst_col, ps_a[:, 0:1], sm, ac,
                                                      ALU.mult, ALU.add))

        emit_rep(tiny("beta"), col("beta"))
        emit_rep(tiny("beta"), col("nbeta"), scale_mult=-1.0)
        emit_rep(sb_dp[0:1, 128:129], col("lamC"))
        for b in range(7):
            emit_rep(sb_dp[0:1, (1 << b):(1 << b) + 1], col("q0", b), add_const=-1.0)
        S("vector", lambda e: e.tensor_tensor(col("blam_t"), col("lam_t"), col("beta"), ALU.mult))
        S("vector", lambda e: e.tensor_copy(sb_onesb[:, :], sb_ones[:, :]))
        SD("sync", [lambda e: e.dma_start(
            out=sb_scr2[:, 0:896],
            in_=bass.AP(tensor=ind_in.tensor, offset=0,
                        ap=[[128, 128], [128 * 128, 7], [1, 128]]))])
        S("vector", lambda e: e.memset(sb_scr[:, 0:128], 1.0))
        for b in range(7):
            def qb(e, b=b):
                return e.tensor_scalar(sb_scr[:, 128:256], sb_scr2[:, 128 * b:128 * (b + 1)],
                                       col("q0", b), 1.0, ALU.mult, ALU.add)
            S("vector", qb)
            S("vector", lambda e: e.tensor_tensor(sb_scr[:, 0:128], sb_scr[:, 0:128],
                                                  sb_scr[:, 128:256], ALU.mult))
        SD("sync", [lambda e: e.dma_start(out=sb_scr[:, 256:384], in_=strict_in),
                    lambda e: e.dma_start(out=sb_scr[:, 384:512], in_=incl_in)])
        S("vector", lambda e: e.tensor_tensor(sb_scr[:, 512:640], sb_scr[:, 0:128],
                                              sb_scr[:, 256:384], ALU.mult))
        S("vector", lambda e: e.tensor_scalar(sb_mD[:, 0:128], sb_scr[:, 512:640],
                                              col("nbeta"), None, ALU.mult))
        S("vector", lambda e: e.tensor_tensor(sb_scr[:, 512:640], sb_scr[:, 0:128],
                                              sb_scr[:, 384:512], ALU.mult))
        S("vector", lambda e: e.tensor_copy(sb_mA[:, 0:128], sb_scr[:, 512:640]))
        for h in range(1, NH):
            S("vector", lambda e, h=h: e.tensor_copy(sb_mD[:, 128 * h:128 * (h + 1)], sb_mD[:, 0:128]))
            S("vector", lambda e, h=h: e.tensor_copy(sb_mA[:, 128 * h:128 * (h + 1)], sb_mA[:, 0:128]))
        setup_all = sc.barrier()
        dp_stack.close()

        if stop_after == "S":
            z1 = G("vector", lambda e: e.memset(sb_scr[:, :], 0.0), waits=setup_all)
            toks = []
            for c in range(NT):
                toks.append(GD("gpsimd", [lambda e, c=c: e.dma_start(
                    out=out_t[128 * c:128 * (c + 1), :], in_=sb_scr[:, :])],
                    waits=(z1,), key="sto0"))
            G("gpsimd", lambda e: e.nop(), waits=(toks[-1],))
            with nc.Block() as block:
                sc.finalize(block)
            return nc

        coll1_tok = coll3_tok = coll2_tok = None

        with ExitStack() as phAB:
            sb_xT = alloc(phAB, "xactT", [128, NKC * T], BF16)
            sb_WT = alloc(phAB, "WT", [128, NKC * DM], BF16)
            sb_xbuf = alloc(phAB, "xbuf", [128, 2 * DM], F32)
            sb_xtok = alloc(phAB, "xtokt", [128, 2 * DM], BF16)
            sb_xn = alloc(phAB, "xnscr", [128, 2 * DM], F32)
            sb_st8 = alloc(phAB, "statscr", [128, 32], F32)
            sb_max = alloc(phAB, "maxacc", [128, 2], F32)
            sb_wtb = alloc(phAB, "wtb", [128, 2 * DM], BF16)
            sb_junkA = alloc(phAB, "junkA", [128, 1024], BF16)
            sb_junkW = alloc(phAB, "junkW", [128, 2 * 1024], BF16)
            sb_wq = alloc(phAB, "wqstage", [128, 2 * DM], F32)
            sb_wlev = alloc(phAB, "wqlev", [128, 2 * DM], BF16)
            sb_wint = alloc(phAB, "wqint", [128, 2 * DM], I32)

            mx_t = G("vector", lambda e: e.memset(sb_max[:, :], 0.0), waits=setup_all)
            wql = [None, None]
            wqsh = [None, None]
            wqcs = [None, None]
            wqigp = [None, None]
            wqstp = [None, None]
            wqsp = [None, None]

            def emit_wq_back(wb):
                q = wb % 2
                wqb = sb_wq[:, DM * q:DM * (q + 1)]
                wlv = sb_wlev[:, DM * q:DM * (q + 1)]
                jw = sb_junkW[:, 1024 * q:1024 * (q + 1)]
                wib = sb_wint[:, DM * q:DM * (q + 1)]
                s8w = 24 + 4 * q
                wqig = G("vector", lambda e: e.tensor_tensor(
                    jw, wib, wqb, ALU.is_gt), waits=(wqcs[q],))
                wqigp[q] = wqig
                wqst = G("vector", lambda e: e.scalar_tensor_tensor(
                    wlv, wib, 128.0, jw, ALU.subtract, ALU.subtract),
                    waits=(wqig, wqsp[q]))
                wqstp[q] = wqst
                wqsc = G("scalar", lambda e: e.activation(
                    wlv, wlv, AF.Identity, scale=sb_st8[:, s8w + 3:s8w + 4]),
                    waits=(wqst,))
                wqsp[q] = GD("gpsimd", [lambda e, wb=wb: e.dma_start(
                    out=wlev_sp[wb], in_=wlv)], waits=(wqsc,), key="wq" + str(q))

            ld = [None, None]
            sq = [None, None]
            xn_t = [None, None]
            tanh_t = [None, None]
            tpose_t = [None, None]
            evT = [None, None]
            spk = [None, None]
            spT_t = [None, None]
            dbg_xn_tok = None
            ld[0] = GD("sync", [lambda e: e.dma_start(out=sb_xbuf[:, 0:DM], in_=x_in[0:128, :])],
                       waits=setup_all, key="lda0")
            for c in range(NT):
                p = c % 2
                xb = sb_xbuf[:, DM * p:DM * (p + 1)]
                xnb = sb_xn[:, DM * p:DM * (p + 1)]
                xtk = sb_xtok[:, DM * p:DM * (p + 1)]
                s8 = 16 * p
                if c + 1 < NT:
                    q = (c + 1) % 2
                    ld[q] = GD("sync", [lambda e, c=c, q=q: e.dma_start(
                        out=sb_xbuf[:, DM * q:DM * (q + 1)],
                        in_=x_in[128 * (c + 1):128 * (c + 2), :])],
                        waits=(xn_t[q], sq[q]), key="lda" + str(q))
                sq[p] = G("scalar", lambda e, xb=xb, s8=s8: e.activation(
                    sb_junkA[:, :], xb, AF.Square, accum_out=sb_st8[:, s8:s8 + 1]),
                    waits=(ld[p], tanh_t[p]))
                v1 = G("vector", lambda e, xb=xb, s8=s8: e.reduce_sum(
                    sb_st8[:, s8 + 1:s8 + 2], xb, axis=AX.X), waits=(ld[p], xn_t[p]))
                v2 = G("vector", lambda e, s8=s8, c=c: e.tensor_scalar_mul(
                    sb_stats[:, 2 * c:2 * c + 1], sb_st8[:, s8 + 1:s8 + 2], 1.0 / DM),
                    waits=(v1,))
                v3_ = G("vector", lambda e, s8=s8, c=c: e.tensor_tensor(
                    sb_st8[:, s8 + 2:s8 + 3], sb_stats[:, 2 * c:2 * c + 1],
                    sb_stats[:, 2 * c:2 * c + 1], ALU.mult), waits=(v2,))
                v4 = G("vector", lambda e, s8=s8: e.scalar_tensor_tensor(
                    sb_st8[:, s8 + 3:s8 + 4], sb_st8[:, s8 + 0:s8 + 1],
                    1.0 / DM, sb_st8[:, s8 + 2:s8 + 3], ALU.mult, ALU.subtract),
                    waits=(v3_, sq[p]))
                v5 = G("vector", lambda e, s8=s8: e.tensor_scalar_add(
                    sb_st8[:, s8 + 3:s8 + 4], sb_st8[:, s8 + 3:s8 + 4], 1e-5), waits=(v4,))
                a1 = G("scalar", lambda e, s8=s8: e.activation(
                    sb_st8[:, s8 + 4:s8 + 5], sb_st8[:, s8 + 3:s8 + 4], AF.Sqrt), waits=(v5,))
                v6 = G("vector", lambda e, s8=s8: e.reciprocal(
                    sb_st8[:, s8 + 5:s8 + 6], sb_st8[:, s8 + 4:s8 + 5]), waits=(a1,))
                v7 = G("vector", lambda e, s8=s8: e.tensor_tensor(
                    sb_st8[:, s8 + 6:s8 + 7], sb_st8[:, s8 + 5:s8 + 6],
                    sb_st8[:, s8 + 5:s8 + 6], ALU.mult), waits=(v6,))
                v8 = G("vector", lambda e, s8=s8: e.tensor_tensor(
                    sb_st8[:, s8 + 6:s8 + 7], sb_st8[:, s8 + 6:s8 + 7],
                    sb_st8[:, s8 + 3:s8 + 4], ALU.mult), waits=(v7,))
                v9 = G("vector", lambda e, s8=s8: e.tensor_scalar(
                    sb_st8[:, s8 + 6:s8 + 7], sb_st8[:, s8 + 6:s8 + 7],
                    -0.5, 1.5, ALU.mult, ALU.add), waits=(v8,))
                v10 = G("vector", lambda e, s8=s8, c=c: e.tensor_tensor(
                    sb_stats[:, 2 * c + 1:2 * c + 2], sb_st8[:, s8 + 5:s8 + 6],
                    sb_st8[:, s8 + 6:s8 + 7], ALU.mult), waits=(v9,))
                xn_t[p] = G("vector", lambda e, xb=xb, xnb=xnb, c=c: e.tensor_scalar(
                    xnb, xb, sb_stats[:, 2 * c:2 * c + 1], sb_stats[:, 2 * c + 1:2 * c + 2],
                    ALU.subtract, ALU.mult), waits=(v10, tanh_t[p], dbg_xn_tok))
                mx_t = G("vector", lambda e, xnb=xnb: e.reduce_max(
                    sb_max[:, 1:2], xnb, axis=AX.X, apply_absolute_value=True),
                    waits=(xn_t[p], mx_t))
                mx_t = G("gpsimd", lambda e: e.tensor_scalar(
                    sb_max[:, 0:1], sb_max[:, 1:2], sb_max[:, 0:1], None, ALU.max),
                    waits=(mx_t,))
                tanh_t[p] = G("scalar", lambda e, xnb=xnb, xtk=xtk: e.activation(
                    xtk, xnb, AF.Tanh), waits=(xn_t[p], spk[p], tpose_t[p]))
                if debug_outs and c == 0:
                    dbg_xn_tok = GD("gpsimd", [lambda e: e.dma_start(out=dbg["d_xn"], in_=sb_xn[:, 0:DM])],
                                    waits=(xn_t[p],), key="dbgs")
                spk[p] = GD("gpsimd", [lambda e, c=c, xtk=xtk: e.dma_start(
                    out=xacttok_sp[c], in_=xtk)], waits=(tanh_t[p],), key="spk" + str(p))
                def tmms(e, xtk=xtk, p=p):
                    r = None
                    for fc in range(NKC):
                        r = e.matmul(ps_a[:, 1024 * p + 128 * fc:1024 * p + 128 * (fc + 1)],
                                     xtk[:, 128 * fc:128 * (fc + 1)],
                                     sb_i128b[:, :], start=True, stop=True)
                    return r
                tpose_t[p] = G("tensor", tmms,
                               waits=(tanh_t[p], evT[p]) + (setup_all if c < 2 else ()))
                evT[p] = G("vector", lambda e, c=c, p=p: e.tensor_copy(
                    bass.AP(tensor=sb_xT, offset=c * 128,
                            ap=[[NKC * T, 128], [T, NKC], [1, 128]]),
                    ps_a[:, 1024 * p:1024 * p + 1024]), waits=(tpose_t[p], spT_t[p]))
                spT_t[p] = GD("gpsimd", [lambda e, c=c: e.dma_start(
                    out=xactT_sp[c],
                    in_=bass.AP(tensor=sb_xT, offset=c * 128,
                                ap=[[NKC * T, 128], [T, NKC], [1, 128]]))],
                    waits=(evT[p],), key="spT" + str(p))
                for wb in range((2 * NIB * c) // NT, (2 * NIB * (c + 1)) // NT):
                    q = wb % 2
                    wqb = sb_wq[:, DM * q:DM * (q + 1)]
                    s8w = 24 + 4 * q
                    w_src = wg_in if wb < NIB else wu_in
                    r_src = wb if wb < NIB else wb - NIB
                    wql[q] = GD("sync", [lambda e, w_src=w_src, r_src=r_src, wqb=wqb:
                                         e.dma_start(out=wqb,
                                                     in_=w_src[128 * r_src:128 * (r_src + 1), :])],
                                waits=(wqigp[q],) + (setup_all if wb < 2 else ()),
                                key="ld2" + str(q))
                    wqr1 = G("vector", lambda e, wqb=wqb, s8w=s8w: e.reduce_max(
                        sb_st8[:, s8w:s8w + 1], wqb, axis=AX.X,
                        apply_absolute_value=True), waits=(wql[q],))
                    wqr2 = G("vector", lambda e, s8w=s8w: e.reciprocal(
                        sb_st8[:, s8w + 1:s8w + 2], sb_st8[:, s8w:s8w + 1]), waits=(wqr1,))
                    wqr3 = G("gpsimd", lambda e, s8w=s8w: e.tensor_scalar_mul(
                        sb_st8[:, s8w + 2:s8w + 3], sb_st8[:, s8w + 1:s8w + 2], 127.0),
                        waits=(wqr2,))
                    wqr4 = G("gpsimd", lambda e, s8w=s8w: e.tensor_scalar_mul(
                        sb_st8[:, s8w + 3:s8w + 4], sb_st8[:, s8w:s8w + 1], 1.0 / 127.0),
                        waits=(wqr1,))
                    wqsh[q] = G("scalar", lambda e, wqb=wqb, s8w=s8w: e.activation(
                        wqb, wqb, AF.Identity, bias=col("c1285"),
                        scale=sb_st8[:, s8w + 2:s8w + 3]), waits=(wqr3, wqr4, wqigp[q]))
                    wqcs[q] = G("scalar", lambda e, wqb=wqb, q=q: e.activation(
                        sb_wint[:, DM * q:DM * (q + 1)], wqb, AF.Identity),
                        waits=(wqsh[q], wqstp[q]))
                    if wb > 0:
                        emit_wq_back(wb - 1)
                if c == NT - 1:
                    emit_wq_back(2 * NIB - 1)

            if stop_after == "A":
                z1 = G("vector", lambda e: e.memset(sb_scr[:, :], 0.0),
                       waits=(mx_t, evT[0], evT[1], spk[0], spk[1], spT_t[0], spT_t[1],
                              tanh_t[0], tanh_t[1], sq[0], sq[1]))
                toks = []
                for c in range(NT):
                    toks.append(GD("gpsimd", [lambda e, c=c: e.dma_start(
                        out=out_t[128 * c:128 * (c + 1), :], in_=sb_scr[:, :])],
                        waits=(z1,), key="sto0"))
                G("gpsimd", lambda e: e.nop(), waits=(toks[-1],))
                with nc.Block() as block:
                    sc.finalize(block)
                return nc

            gm1 = G("tensor", lambda e: e.matmul(ps_a[0:1, 0:128], sb_max[:, 0:1],
                                                 sb_i128f[:, :], start=True, stop=True),
                    waits=(mx_t, evT[0], evT[1]))
            gm2 = G("vector", lambda e: e.tensor_copy(sb_scr[0:1, 0:128], ps_a[0:1, 0:128]),
                    waits=(gm1,))
            gm3 = G("vector", lambda e: e.reduce_max(tiny("gmax"), sb_scr[0:1, 0:128], axis=AX.X),
                    waits=(gm2,))
            gm4 = G("vector", lambda e: e.memset(sb_tiny[0:1, TI["pad"]:TI["pad"] + 16], 0.0),
                    waits=(gm3,))
            gm5 = G("vector", lambda e: e.tensor_copy(sb_tiny[0:1, TI["pad"]:TI["pad"] + 1],
                                                      tiny("gmax")), waits=(gm4,))
            gm6 = GD("gpsimd", [lambda e: e.dma_start(out=cb1_i[0:16],
                                                      in_=sb_tiny[0:1, TI["pad"]:TI["pad"] + 16])],
                     waits=(gm5,), key="dma_g")
            coll1_tok = G("gpsimd", lambda e: e.collective_compute(
                "AllReduce", ALU.max, replica_groups=[list(range(n_cores))],
                ins=[cb1_i.tensor.ap().opt()], outs=[cb1_o.tensor.ap().opt()]),
                waits=(gm6,), key="cc1")

            wld = [None, None]
            wcast = [None, None]
            wtp = [None, None]
            wev = [None, None]
            for r in range(NKC):
                p = r % 2
                dst = sb_scr if p == 0 else sb_scr2
                wld[p] = GD("sync", [lambda e, r=r, dst=dst: e.dma_start(
                    out=dst[:, :], in_=W_in[128 * r:128 * (r + 1), :])],
                    waits=(wcast[p],) + ((gm3, setup_all[1]) if r < 2 else ()), key="wW" + str(p))
                wcast[p] = G("vector", lambda e, dst=dst, p=p: e.tensor_copy(
                    sb_wtb[:, DM * p:DM * (p + 1)], dst[:, :]), waits=(wld[p], wtp[p]))
                def wtmm(e, p=p):
                    rr = None
                    for fc in range(NKC):
                        rr = e.matmul(ps_b[:, 1024 * p + 128 * fc:1024 * p + 128 * (fc + 1)],
                                      sb_wtb[:, DM * p + 128 * fc:DM * p + 128 * (fc + 1)],
                                      sb_i128b[:, :], start=True, stop=True)
                    return rr
                wtp[p] = G("tensor", wtmm, waits=(wcast[p], wev[p]))
                wev[p] = G("vector", lambda e, r=r, p=p: e.tensor_copy(
                    bass.AP(tensor=sb_WT, offset=r * 128,
                            ap=[[NKC * DM, 128], [DM, NKC], [1, 128]]),
                    ps_b[:, 1024 * p:1024 * p + 1024]), waits=(wtp[p],))
            wprep_done = (wev[0], wev[1])

            if stop_after == "W":
                z1 = G("vector", lambda e: e.memset(sb_scr[:, :], 0.0),
                       waits=wprep_done + (coll1_tok, mx_t, evT[0], evT[1], spk[0], spk[1],
                                           spT_t[0], spT_t[1], sq[0], sq[1]))
                toks = []
                for c in range(NT):
                    toks.append(GD("gpsimd", [lambda e, c=c: e.dma_start(
                        out=out_t[128 * c:128 * (c + 1), :], in_=sb_scr[:, :])],
                        waits=(z1,), key="sto0"))
                G("gpsimd", lambda e: e.nop(), waits=(toks[-1],))
                with nc.Block() as block:
                    sc.finalize(block)
                return nc

            sb_Y = alloc(phAB, "Ytile", [128, NH * 128], BF16)
            sb_X0 = alloc(phAB, "X0", [128, NH * 128], BF16)
            sb_X1 = alloc(phAB, "X1", [128, NH * 128], BF16)
            sb_dmt = alloc(phAB, "dmt", [128, NH * 128], BF16)
            sb_rt = alloc(phAB, "rtT", [64, NH * 128], BF16)
            sb_dk = alloc(phAB, "dkt", [128, DM], BF16)
            sb_u0 = alloc(phAB, "u0t", [128, NH * DH], BF16)
            sb_stat = alloc(phAB, "stattile", [128, DM], BF16)
            sb_xk = alloc(phAB, "xkstream", [128, 2 * DM], BF16)

            s0_ev = G("vector", lambda e: e.memset(sb_S0[:, :], 0.0), waits=setup_all)
            ldk = [None, None]
            ldk[0] = GD("sync", [lambda e: e.dma_start(out=sb_xk[:, 0:DM], in_=xacttok_sp[0])],
                        waits=(spk[0],), key="ldk0")
            spst_t = sppr_t = sprt_t = None
            st_ev = gr_ev = rt_ev = u0_ev = dk_t = yv_t = yk_t = None
            dbg_st_tok = None
            x_ev = [None, None, None]
            nt_lim = NT
            bsub = 99
            if stop_after and stop_after.startswith("B") and len(stop_after) > 1:
                parts = stop_after[1:].split(".")
                nt_lim = int(parts[0])
                if len(parts) > 1:
                    bsub = int(parts[1])
            for c in range(nt_lim):
                p = c % 2
                xkc = sb_xk[:, DM * p:DM * (p + 1)]
                if c + 1 < nt_lim:
                    q = (c + 1) % 2
                    ldk[q] = GD("sync", [lambda e, c=c, q=q: e.dma_start(
                        out=sb_xk[:, DM * q:DM * (q + 1)], in_=xacttok_sp[c + 1])],
                        waits=(spk[(c + 1) % 2], yk_t, dk_t), key="ldk" + str(q))
                def smms(e, c=c):
                    r = None
                    for half in range(2):
                        for fc in range(NKC):
                            r = e.matmul(ps_b[:, 512 * half:512 * (half + 1)],
                                         sb_xT[:, fc * T + c * 128: fc * T + (c + 1) * 128],
                                         sb_WT[:, fc * DM + 512 * half: fc * DM + 512 * (half + 1)],
                                         start=(fc == 0), stop=(fc == NKC - 1))
                    return r
                st_mm = G("tensor", smms,
                          waits=wprep_done + (evT[p], x_ev[2], u0_ev, st_ev))
                st_ev = G("vector", lambda e: e.tensor_copy(sb_stat[:, :], ps_b[:, 0:1024]),
                          waits=(st_mm, spst_t, yv_t, dbg_st_tok))
                spst_t = GD("gpsimd", [lambda e, c=c: e.dma_start(out=static_sp[c],
                                                                  in_=sb_stat[:, :])],
                            waits=(st_ev,), key="spst")
                if debug_outs and c == 0:
                    dbg_st_tok = GD("gpsimd", [lambda e: e.dma_start(out=dbg["d_static"], in_=sb_stat[:, :])],
                                    waits=(st_ev,), key="dbgs")
                if bsub <= 1:
                    break
                def gmms(e, c=c):
                    r = None
                    for h in [hh for hh in range(NH) if hh % 2 == 0] + \
                             [hh for hh in range(NH) if hh % 2 == 1]:
                        fc, po = h // 2, 64 * (h % 2)
                        sl = sb_xT[po:po + 64, fc * T + c * 128: fc * T + (c + 1) * 128]
                        r = e.matmul(ps_a[:, 128 * h:128 * (h + 1)], sl, sl,
                                     start=True, stop=True, tile_position=(po, 0))
                    return r
                gr_mm = G("tensor", gmms, waits=(evT[p], gr_ev, x_ev[1], rt_ev, s0_ev))
                gr_ev = G("vector", lambda e: e.tensor_tensor(sb_dmt[:, :], ps_a[:, 0:2048],
                                                              sb_mD[:, :], ALU.mult),
                          waits=(gr_mm,))
                if bsub <= 2:
                    break
                yv_t = G("vector", lambda e: e.tensor_scalar(
                    v3(sb_Y, 0, NH * 128, NH, 128, 64),
                    v3(sb_stat, 0, DM, NH, 64, 64),
                    col("beta"), None, ALU.mult), waits=(st_ev, x_ev[0], yk_t))
                yk_t = G("vector", lambda e, p=p: e.tensor_scalar(
                    v3(sb_Y, 64, NH * 128, NH, 128, 64),
                    bass.AP(tensor=sb_xk, offset=DM * p,
                            ap=[[2 * DM, 128], [64, NH], [1, 64]]),
                    col("blam_t"), None, ALU.mult), waits=(ldk[p], yv_t))
                if bsub <= 3:
                    break
                xprev = sb_Y
                prevtok = (yk_t,)
                for j in range(NEU):
                    pst = ps_b if (j % 2 == 0) else ps_a
                    xnew = (sb_X0, sb_X1, sb_X0)[j]
                    def nmms(e, pst=pst, xprev=xprev):
                        r = None
                        for h in range(NH):
                            r = e.matmul(pst[:, 128 * h:128 * (h + 1)],
                                         sb_dmt[:, 128 * h:128 * (h + 1)],
                                         xprev[:, 128 * h:128 * (h + 1)],
                                         start=True, stop=True)
                        return r
                    wx = (gr_ev,) + prevtok
                    if j == 0:
                        wx = wx + (st_ev, sppr_t, u0_ev)
                    elif j == 2:
                        wx = wx + (x_ev[0], sppr_t, u0_ev)
                    nm = G("tensor", nmms, waits=wx)
                    x_ev[j] = G("vector", lambda e, pst=pst, xnew=xnew: e.tensor_tensor(
                        xnew[:, :], pst[:, 0:2048], sb_Y[:, :], ALU.add), waits=(nm,))
                    xprev = xnew
                    prevtok = (x_ev[j],)
                X4 = xprev
                sppr_t = GD("gpsimd", [lambda e, c=c, X4=X4: e.dma_start(
                    out=pr_sp[c], in_=v3(X4, 0, NH * 128, NH, 128, 64))],
                            waits=(x_ev[NEU - 1],), key="sppr")
                if bsub <= 4:
                    break
                def rmms(e, X4=X4):
                    r = None
                    for h in range(NH):
                        r = e.matmul(ps_a[0:64, 128 * h:128 * (h + 1)],
                                     X4[:, 128 * h + 64:128 * (h + 1)],
                                     sb_i128b[:, :], start=True, stop=True)
                    return r
                rt_mm = G("tensor", rmms, waits=(x_ev[NEU - 1], gr_ev))
                rt_ev = G("vector", lambda e: e.tensor_copy(sb_rt[:, :], ps_a[0:64, 0:2048]),
                          waits=(rt_mm, sprt_t, u0_ev))
                sprt_t = GD("gpsimd", [lambda e, c=c: e.dma_start(out=rt_sp[c], in_=sb_rt[:, :])],
                            waits=(rt_ev,), key="sprt")
                if bsub <= 5:
                    break
                dk_t = G("scalar", lambda e, xkc=xkc: e.activation(
                    sb_dk[:, :], xkc, AF.Identity, scale=col("lam_ct")),
                    waits=(ldk[p], s0_ev, dk_t))
                def u0mms(e):
                    r = None
                    for h in range(NH):
                        r = e.matmul(ps_b[:, 64 * h:64 * (h + 1)],
                                     sb_rt[:, 128 * h:128 * (h + 1)],
                                     sb_S0[:, 64 * h:64 * (h + 1)],
                                     start=True, stop=True)
                    return r
                u0_mm = G("tensor", u0mms, waits=(rt_ev, s0_ev, x_ev[0]))
                u0_ev = G("vector", lambda e, X4=X4: e.scalar_tensor_tensor(
                    v3(sb_u0, 0, NH * DH, NH, 64, 64),
                    v3(ps_b, 0, 2048, NH, 64, 64), -1.0,
                    v3(X4, 0, NH * 128, NH, 128, 64),
                    ALU.mult, ALU.add), waits=(u0_mm,))
                if bsub <= 6:
                    break
                def s0mms(e):
                    r = None
                    for h in range(NH):
                        r = e.matmul(ps_a[0:64, 1024 + 64 * h:1024 + 64 * (h + 1)],
                                     sb_dk[:, 64 * h:64 * (h + 1)],
                                     sb_u0[:, 64 * h:64 * (h + 1)],
                                     start=True, stop=True)
                    return r
                s0_mm = G("tensor", s0mms, waits=(dk_t, u0_ev, gr_ev))
                s0_ev = G("vector", lambda e: e.scalar_tensor_tensor(
                    sb_S0[:, :], sb_S0[:, :], sb_cols[0:64, COL["lamC"]:COL["lamC"] + 1],
                    ps_a[0:64, 1024:2048], ALU.mult, ALU.add), waits=(s0_mm,))
            cb3a = G("vector", lambda e: e.tensor_copy(sb_scr[0:64, 0:1024], sb_S0[:, :]),
                     waits=(s0_ev, gr_ev, st_ev, yk_t, dk_t, u0_ev, rt_ev))
            cb3b = GD("gpsimd", [lambda e: e.dma_start(
                out=bass.AP(tensor=cb3_i.tensor, offset=0, ap=[[1024, 64], [1, 1024]]),
                in_=sb_scr[0:64, 0:1024])], waits=(cb3a,), key="dma_g")
            coll3_tok = G("gpsimd", lambda e: e.collective_compute(
                "AllGather", ALU.bypass, replica_groups=[list(range(n_cores))],
                ins=[cb3_i.tensor.ap().opt()], outs=[cb3_o.tensor.ap().opt()]),
                waits=(cb3b,), key="cc3")
            phB_done = sc.barrier()

        if stop_after is not None and stop_after.startswith("B"):
            z1 = G("vector", lambda e: e.memset(sb_scr[:, :], 0.0), waits=phB_done)
            toks = []
            for c in range(NT):
                toks.append(GD("gpsimd", [lambda e, c=c: e.dma_start(
                    out=out_t[128 * c:128 * (c + 1), :], in_=sb_scr[:, :])],
                    waits=(z1,), key="sto0"))
            G("gpsimd", lambda e: e.nop(), waits=(toks[-1],))
            with nc.Block() as block:
                sc.finalize(block)
            return nc

        with ExitStack() as phC:
            sb_xlT = alloc(phC, "xlevT", [128, NKC * T], BF16)
            sb_wgT = alloc(phC, "wgT", [128, NKC * IFF], BF16)
            sb_wuT = alloc(phC, "wuT", [128, NKC * IFF], BF16)
            sb_xbuf2 = alloc(phC, "xbuf2", [128, 2 * DM], F32)
            sb_u2 = alloc(phC, "ushift", [128, 2 * DM], F32)
            sb_int2 = alloc(phC, "int2", [128, DM], I32)
            sb_lev2 = alloc(phC, "lev2", [128, 2 * DM], BF16)
            sb_hbuf = alloc(phC, "hbuf", [128, 2 * 512], BF16)
            sb_silu = alloc(phC, "silu", [128, 512], F32)
            sb_hmax = alloc(phC, "hmax", [128, 2], F32)
            sb_st8c = alloc(phC, "statscrC", [128, 16], F32)
            sb_junkC = alloc(phC, "junkC", [128, 1024], BF16)

            pb = phB_done
            c1a = GD("sync", [lambda e: e.dma_start(out=sb_tiny[0:1, TI["pad"]:TI["pad"] + 16],
                                                    in_=cb1_o[0:16])],
                     waits=pb + (coll1_tok,), key="dma_s")
            c1b = G("vector", lambda e: e.tensor_copy(tiny("gmax"),
                                                      sb_tiny[0:1, TI["pad"]:TI["pad"] + 1]),
                    waits=(c1a,))
            if debug_outs:
                GD("gpsimd", [lambda e: e.dma_start(out=dbg["d_gmax"], in_=tiny("gmax"))],
                   waits=(c1b,), key="dbgs")
            c1c = G("vector", lambda e: e.tensor_tensor(tiny("t1"), tiny("sigcx"), tiny("gmax"),
                                                        ALU.mult), waits=(c1b,))
            c1d = G("vector", lambda e: e.tensor_scalar_mul(tiny("scalex"), tiny("t1"),
                                                            1.0 / 127.0), waits=(c1c,))
            c1e = G("vector", lambda e: e.reciprocal(tiny("t2"), tiny("t1")), waits=(c1d,))
            c1f = G("vector", lambda e: e.tensor_scalar_mul(tiny("invx"), tiny("t2"), 127.0),
                    waits=(c1e,))
            r1m = G("tensor", lambda e: e.matmul(ps_a[:, 0:1], sb_ones[0:1, :], tiny("invx"),
                                                 start=True, stop=True), waits=(c1f,) + pb)
            rx1 = G("vector", lambda e: e.tensor_copy(col("invx"), ps_a[:, 0:1]), waits=(r1m,))
            r2m = G("tensor", lambda e: e.matmul(ps_a[:, 0:1], sb_ones[0:1, :], tiny("scalex"),
                                                 start=True, stop=True), waits=(rx1,))
            rx2 = G("vector", lambda e: e.tensor_copy(col("scalex"), ps_a[:, 0:1]), waits=(r2m,))

            wl = [None, None]
            wsh = [None, None]
            wcl = [None, None]
            wcs = [None, None]
            wig = [None, None]
            wst = [None, None]
            wtp2 = [None, None]
            wev2 = [None, None]
            wred = [None, None]

            def quant_tile(p, src_tok, scale_col, bias_col, dst_emit, first_waits=()):
                ub = sb_u2[:, DM * p:DM * (p + 1)]
                ib2 = sb_int2[:, :]
                cb = sb_junkC[:, :]
                lb = sb_lev2[:, DM * p:DM * (p + 1)]
                xb2 = sb_xbuf2[:, DM * p:DM * (p + 1)]
                wsh[p] = G("scalar", lambda e: e.activation(
                    ub, xb2, AF.Identity, bias=bias_col, scale=scale_col),
                    waits=(src_tok, wcl[p], wig[p]) + first_waits)
                wcl[p] = G("vector", lambda e: e.tensor_scalar(
                    ub, ub, 255.99, 0.01, ALU.min, ALU.max), waits=(wsh[p], wcs[p]))
                wcs[p] = G("scalar", lambda e: e.activation(ib2, ub, AF.Identity),
                           waits=(wcl[p], wst[0], wst[1]))
                wig[p] = G("vector", lambda e: e.tensor_tensor(cb, ib2, ub, ALU.is_gt),
                           waits=(wcs[p], wst[0], wst[1]))
                wst[p] = G("vector", lambda e: e.scalar_tensor_tensor(
                    lb, ib2, 128.0, cb, ALU.subtract, ALU.subtract),
                    waits=(wig[p], wtp2[p]))
                def wtmm2(e, p=p, lb=lb):
                    rr = None
                    for fc in range(NKC):
                        rr = e.matmul(ps_b[:, 1024 * p + 128 * fc:1024 * p + 128 * (fc + 1)],
                                      lb[:, 128 * fc:128 * (fc + 1)],
                                      sb_i128b[:, :], start=True, stop=True)
                    return rr
                wtp2[p] = G("tensor", wtmm2, waits=(wst[p], wev2[p]) + first_waits)
                wev2[p] = G("scalar", dst_emit, waits=(wtp2[p],))
                return wev2[p]

            for r in range(2 * NIB):
                p = r % 2
                lb = sb_lev2[:, DM * p:DM * (p + 1)]
                dst_T = sb_wgT if r < NIB else sb_wuT
                rr = r if r < NIB else r - NIB
                wl[p] = GD("sync", [lambda e, r=r, lb=lb: e.dma_start(
                    out=lb, in_=wlev_sp[r])],
                    waits=(wtp2[p],) + (pb if r < 2 else ()), key="ld2" + str(p))
                def wtmm2w(e, p=p, lb=lb):
                    rr2 = None
                    for fc in range(NKC):
                        rr2 = e.matmul(ps_b[:, 1024 * p + 128 * fc:1024 * p + 128 * (fc + 1)],
                                       lb[:, 128 * fc:128 * (fc + 1)],
                                       sb_i128b[:, :], start=True, stop=True)
                    return rr2
                wtp2[p] = G("tensor", wtmm2w, waits=(wl[p], wev2[p]) + (pb if r < 2 else ()))
                def dste(e, rr=rr, p=p, dst_T=dst_T):
                    return e.activation(
                        bass.AP(tensor=dst_T, offset=rr * 128,
                                ap=[[NKC * IFF, 128], [IFF, NKC], [1, 128]]),
                        ps_b[:, 1024 * p:1024 * p + 1024], AF.Identity)
                wev2[p] = G("scalar", dste, waits=(wtp2[p],))
            wgu_done = (wev2[0], wev2[1])

            sc_t = [None, None]
            for c in range(NT):
                p = c % 2
                xb2 = sb_xbuf2[:, DM * p:DM * (p + 1)]
                xlt = GD("sync", [lambda e, c=c, xb2=xb2: e.dma_start(
                    out=xb2, in_=x_in[128 * c:128 * (c + 1), :])],
                    waits=(wsh[p], wred[p]) if c < 2 else (wsh[p],), key="ld2" + str(p))
                s1 = G("gpsimd", lambda e, c=c, p=p: e.tensor_tensor(
                    sb_st8c[:, 8 * p + 3:8 * p + 4], sb_stats[:, 2 * c + 1:2 * c + 2],
                    col("invx"), ALU.mult), waits=(rx1, sc_t[p], wsh[p]))
                s2 = G("gpsimd", lambda e, c=c, p=p: e.tensor_tensor(
                    sb_st8c[:, 8 * p + 4:8 * p + 5], sb_stats[:, 2 * c:2 * c + 1],
                    sb_st8c[:, 8 * p + 3:8 * p + 4], ALU.mult), waits=(s1,))
                sc_t[p] = G("gpsimd", lambda e, p=p: e.tensor_scalar(
                    sb_st8c[:, 8 * p + 4:8 * p + 5], sb_st8c[:, 8 * p + 4:8 * p + 5],
                    -1.0, 128.5, ALU.mult, ALU.add), waits=(s2,))
                def dste(e, c=c, p=p):
                    return e.activation(
                        bass.AP(tensor=sb_xlT, offset=c * 128,
                                ap=[[NKC * T, 128], [T, NKC], [1, 128]]),
                        ps_b[:, 1024 * p:1024 * p + 1024], AF.Identity)
                last = quant_tile(p, xlt, sb_st8c[:, 8 * p + 3:8 * p + 4],
                                  sb_st8c[:, 8 * p + 4:8 * p + 5], dste,
                                  first_waits=(sc_t[p],))
                if debug_outs and c == 0:
                    pass
            xlev_done = (wev2[0], wev2[1])

            hm0 = G("vector", lambda e: e.memset(sb_hmax[:, :], 0.0), waits=pb)
            NTT = T // 512
            regions = [ps_a[:, 0:1024], ps_a[:, 1024:2048], ps_b[:, 0:1024], ps_b[:, 1024:2048]]
            silu_t = [None] * 4
            hmul_t = [None] * 4
            hmul_last = None
            red_t = [None, None]
            hst_t = [None, None]
            mxu_t = hm0
            bidx = 0
            for tt_ in range(NTT):
                for ib in range(NIB):
                    q = bidx % 4
                    hp = bidx % 2
                    reg = regions[q]
                    def gmms2(e, ib=ib, tt_=tt_, reg=reg):
                        r = None
                        for which in range(2):
                            wT = sb_wgT if which == 0 else sb_wuT
                            for fc in range(NKC):
                                r = e.matmul(reg[:, 512 * which:512 * (which + 1)],
                                             wT[:, fc * IFF + 128 * ib: fc * IFF + 128 * (ib + 1)],
                                             sb_xlT[:, fc * T + 512 * tt_: fc * T + 512 * (tt_ + 1)],
                                             start=(fc == 0), stop=(fc == NKC - 1))
                        return r
                    base_w = (silu_t[q], hmul_t[q])
                    if bidx < 4:
                        base_w = base_w + wgu_done + xlev_done + (wtp2[0], wtp2[1])
                    gt = G("tensor", gmms2, waits=base_w)
                    silu_t[q] = G("scalar", lambda e, reg=reg: e.activation(
                        sb_silu[:, :], reg[:, 0:512],
                        AF.Silu, scale=col("scalex")),
                        waits=(gt, rx2, hmul_last))
                    hmul_t[q] = G("vector", lambda e, reg=reg, hp=hp: e.scalar_tensor_tensor(
                        sb_hbuf[:, 512 * hp:512 * (hp + 1)], reg[:, 512:1024],
                        col("scalex"),
                        sb_silu[:, :], ALU.mult, ALU.mult),
                        waits=(silu_t[q], rx2, hst_t[hp], red_t[hp]))
                    hmul_last = hmul_t[q]
                    red_t[hp] = G("vector", lambda e, hp=hp: e.reduce_max(
                        sb_hmax[:, 1:2], sb_hbuf[:, 512 * hp:512 * (hp + 1)], axis=AX.X,
                        apply_absolute_value=True), waits=(hmul_t[q], mxu_t))
                    mxu_t = G("gpsimd", lambda e: e.tensor_scalar(
                        sb_hmax[:, 0:1], sb_hmax[:, 1:2], sb_hmax[:, 0:1], None, ALU.max),
                        waits=(red_t[hp], mxu_t))
                    def hsp(e, ib=ib, tt_=tt_, hp=hp):
                        return e.dma_start(
                            out=bass.AP(tensor=h_sp.tensor, offset=(ib * 128) * T + tt_ * 512,
                                        ap=[[T, 128], [1, 512]]),
                            in_=sb_hbuf[:, 512 * hp:512 * (hp + 1)])
                    hst_t[hp] = GD("gpsimd", [hsp], waits=(hmul_t[q],), key="hst" + str(hp))
                    bidx += 1

            h1 = G("tensor", lambda e: e.matmul(ps_a[0:1, 0:128], sb_hmax[:, 0:1],
                                                sb_i128f[:, :], start=True, stop=True),
                   waits=(mxu_t,) + tuple(silu_t) + tuple(hmul_t))
            h2 = G("vector", lambda e: e.tensor_copy(sb_scr[0:1, 0:128], ps_a[0:1, 0:128]),
                   waits=(h1,))
            h3 = G("vector", lambda e: e.reduce_max(tiny("hmax"), sb_scr[0:1, 0:128], axis=AX.X),
                   waits=(h2,))
            h4 = G("vector", lambda e: e.memset(sb_tiny[0:1, TI["pad"]:TI["pad"] + 16], 0.0),
                   waits=(h3,))
            h5 = G("vector", lambda e: e.tensor_copy(sb_tiny[0:1, TI["pad"]:TI["pad"] + 1],
                                                     tiny("hmax")), waits=(h4,))
            h6 = GD("gpsimd", [lambda e: e.dma_start(out=cb2_i[0:16],
                                                     in_=sb_tiny[0:1, TI["pad"]:TI["pad"] + 16])],
                    waits=(h5, hst_t[0], hst_t[1]), key="dma_g")
            coll2_tok = G("gpsimd", lambda e: e.collective_compute(
                "AllReduce", ALU.max, replica_groups=[list(range(n_cores))],
                ins=[cb2_i.tensor.ap().opt()], outs=[cb2_o.tensor.ap().opt()]),
                waits=(h6,), key="cc2")
            phC_done = sc.barrier()

        if stop_after == "C":
            z1 = G("vector", lambda e: e.memset(sb_scr[:, :], 0.0), waits=phC_done + (coll1_tok, coll2_tok, coll3_tok))
            toks = []
            for c in range(NT):
                toks.append(GD("gpsimd", [lambda e, c=c: e.dma_start(
                    out=out_t[128 * c:128 * (c + 1), :], in_=sb_scr[:, :])],
                    waits=(z1,), key="sto0"))
            G("gpsimd", lambda e: e.nop(), waits=(toks[-1],))
            with nc.Block() as block:
                sc.finalize(block)
            return nc

        with ExitStack() as phDE:
            sb_attn = alloc(phDE, "attnring", [128, 6 * DM], BF16)
            sb_gb = alloc(phDE, "gammarep", [128, DM], F32)
            sb_lnbrow = alloc(phDE, "lnbrow", [1, DM], F32)
            sb_brow = alloc(phDE, "betarow", [1, DM], BF16)
            sb_S = alloc(phDE, "S_tile", [64, NH * DH], BF16)
            sb_xTc = alloc(phDE, "xTc", [64, 2 * NH * 128], BF16)
            sb_xkc2 = alloc(phDE, "xkc2", [128, 2 * DM], BF16)
            sb_stc = alloc(phDE, "stc", [128, 2 * DM], BF16)
            sb_prc = alloc(phDE, "prc", [128, 2 * NH * DH], BF16)
            sb_rtc = alloc(phDE, "rtc", [64, 2 * NH * 128], BF16)
            sb_aot = alloc(phDE, "aot", [128, NH * 128], BF16)
            sb_dk2 = alloc(phDE, "dk2", [128, DM], BF16)
            sb_U = alloc(phDE, "Utile", [128, NH * DH], BF16)
            sb_y = alloc(phDE, "ytile", [128, 2 * DM], F32)
            sb_junkDE = alloc(phDE, "junkDE", [128, 1024], BF16)
            sb_wdT = alloc(phDE, "wdT", [128, NIB * DM], BF16)
            sb_hc = alloc(phDE, "hc", [128, 2 * IFF], BF16)
            sb_hu = alloc(phDE, "hu", [128, 2 * IFF], F32)
            sb_hlev = alloc(phDE, "hlev", [128, 2 * IFF], BF16)
            sb_hint = alloc(phDE, "hint", [128, IFF], I32)
            sb_xr = alloc(phDE, "xr", [128, DM], F32)
            sb_outb = alloc(phDE, "outb", [128, DM], F32)
            sb_st8d = alloc(phDE, "st8d", [128, 16], F32)

            pc = phC_done
            g1 = GD("sync", [lambda e: e.dma_start(out=sb_scr[0:1, 0:1024], in_=lng_in)],
                    waits=pc, key="dma_s")
            def repmm_a(e):
                r = None
                for hh in range(2):
                    r = e.matmul(ps_a[:, 512 * hh:512 * (hh + 1)], sb_ones[0:1, :],
                                 sb_scr[0:1, 512 * hh:512 * (hh + 1)], start=True, stop=True)
                return r
            g2 = G("tensor", repmm_a, waits=(g1,) + pc)
            g3 = G("vector", lambda e: e.tensor_copy(sb_gb[:, :], ps_a[:, 0:1024]),
                   waits=(g2,) + pc)
            g4 = GD("sync", [lambda e: e.dma_start(out=sb_lnbrow[:, :], in_=lnb_in)],
                    waits=pc, key="dma_s")

            e7 = GD("sync", [lambda e: e.dma_start(out=sb_tiny[0:1, TI["pad"]:TI["pad"] + 16],
                                                   in_=cb2_o[0:16])],
                    waits=(coll2_tok,) + pc, key="dma_s")
            e8 = G("vector", lambda e: e.tensor_copy(tiny("hmax"),
                                                     sb_tiny[0:1, TI["pad"]:TI["pad"] + 1]),
                   waits=(e7,))
            if debug_outs:
                GD("gpsimd", [lambda e: e.dma_start(out=dbg["d_hmax"], in_=tiny("hmax"))],
                   waits=(e8,), key="dbgs")
            e9 = G("vector", lambda e: e.tensor_tensor(tiny("t1"), tiny("sigcd"),
                                                       tiny("hmax"), ALU.mult), waits=(e8,))
            e10 = G("vector", lambda e: e.tensor_scalar_mul(tiny("scaleh"), tiny("t1"),
                                                            1.0 / 127.0), waits=(e9,))
            e11 = G("vector", lambda e: e.reciprocal(tiny("t2"), tiny("t1")), waits=(e10,))
            e12 = G("vector", lambda e: e.tensor_scalar_mul(tiny("invh"), tiny("t2"), 127.0),
                    waits=(e11,))
            e13 = G("tensor", lambda e: e.matmul(ps_b[:, 0:1], sb_ones[0:1, :], tiny("invh"),
                                                 start=True, stop=True), waits=(e12, g2))
            e14 = G("vector", lambda e: e.tensor_copy(col("invh"), ps_b[:, 0:1]), waits=(e13,))
            e15 = G("tensor", lambda e: e.matmul(ps_b[:, 0:1], sb_ones[0:1, :], tiny("scaleh"),
                                                 start=True, stop=True), waits=(e14,))
            e16 = G("vector", lambda e: e.tensor_copy(col("scaleh"), ps_b[:, 0:1]),
                    waits=(e15,))
            eb1 = G("vector", lambda e: e.tensor_scalar(
                sb_scr[0:1, 0:DM], sb_lnbrow[:, :], tiny("invh"), None, ALU.mult),
                waits=(e12, g4))
            eb2 = G("vector", lambda e: e.tensor_copy(sb_brow[:, :], sb_scr[0:1, 0:DM]),
                    waits=(eb1,))

            d2 = G("vector", lambda e: e.memset(sb_scr[0:64, 0:1024], 0.0), waits=(eb2,))
            prev = d2
            dld = None
            for pair in range((n_cores + 1) // 2):
                nsl = min(2, n_cores - 2 * pair)
                dld = GD("sync", [lambda e, pair=pair, nsl=nsl: e.dma_start(
                    out=sb_y[0:64, 0:1024 * nsl],
                    in_=bass.AP(tensor=cb3_o.tensor, offset=2 * pair * 64 * NH * DH,
                                ap=[[1024, 64], [64 * NH * DH, nsl], [1, 1024]]))],
                    waits=(coll3_tok, prev) + (pc if pair == 0 else ()), key="dma_s")
                for k in range(nsl):
                    jc = 2 * pair + k
                    prev = G("vector", lambda e, jc=jc, k=k: e.scalar_tensor_tensor(
                        sb_scr[0:64, 0:1024], sb_y[0:64, 1024 * k:1024 * (k + 1)],
                        sb_sel[:, jc:jc + 1], sb_scr[0:64, 0:1024], ALU.mult, ALU.add),
                        waits=(prev, dld))
            s_ev = G("vector", lambda e: e.tensor_copy(sb_S[:, :], sb_scr[0:64, 0:1024]),
                     waits=(prev,))
            if debug_outs:
                GD("gpsimd", [lambda e: e.dma_start(out=dbg["d_sstart"],
                                                    in_=sb_scr[0:64, 0:1024])],
                   waits=(s_ev,), key="dbgs")

            ewl = [None, None]
            ewsh = [None, None]
            ewcl = [None, None]
            ewcs = ewig = None
            ewigp = [None, None]
            ewst = [None, None]
            ew_ev = None
            for r in range(DM // 128):
                p = r % 2
                hcb = sb_hu[:, IFF * p:IFF * (p + 1)]
                hlb = sb_hlev[:, IFF * p:IFF * (p + 1)]
                ewl[p] = GD("sync", [lambda e, r=r, hcb=hcb: e.dma_start(
                    out=hcb, in_=wd_in[128 * r:128 * (r + 1), :])],
                    waits=(ewigp[p],) + (pc if r < 2 else ()), key="eht" + str(p))
                ewr1 = G("vector", lambda e, hcb=hcb, p=p: e.reduce_max(
                    sb_st8d[:, 8 * p:8 * p + 1], hcb, axis=AX.X, apply_absolute_value=True),
                    waits=(ewl[p],))
                ewr2 = G("vector", lambda e, r=r, p=p: e.tensor_scalar_mul(
                    sb_wscale[:, 2 * NIB + r:2 * NIB + r + 1], sb_st8d[:, 8 * p:8 * p + 1],
                    1.0 / 127.0), waits=(ewr1,))
                ewr3 = G("vector", lambda e, p=p: e.reciprocal(
                    sb_st8d[:, 8 * p + 1:8 * p + 2], sb_st8d[:, 8 * p:8 * p + 1]),
                    waits=(ewr2,))
                ewr4 = G("vector", lambda e, p=p: e.tensor_scalar_mul(
                    sb_st8d[:, 8 * p + 2:8 * p + 3], sb_st8d[:, 8 * p + 1:8 * p + 2], 127.0),
                    waits=(ewr3,))
                ewsh[p] = G("scalar", lambda e, hcb=hcb, p=p: e.activation(
                    hcb, hcb, AF.Identity, bias=col("c1285"),
                    scale=sb_st8d[:, 8 * p + 2:8 * p + 3]),
                    waits=(ewr4, ewcl[p]))
                ewcs = G("scalar", lambda e, hcb=hcb: e.activation(
                    sb_hint[:, :], hcb, AF.Identity), waits=(ewsh[p], ewig))
                ewig = G("vector", lambda e, hcb=hcb, hlb=hlb: e.tensor_tensor(
                    hlb, sb_hint[:, :], hcb, ALU.is_gt), waits=(ewcs, ewst[p], ew_ev))
                ewigp[p] = ewig
                ews1 = G("vector", lambda e, hlb=hlb: e.scalar_tensor_tensor(
                    hlb, sb_hint[:, :], 128.0, hlb, ALU.subtract, ALU.subtract),
                    waits=(ewig,))
                ewst[p] = G("scalar", lambda e, hlb=hlb, r=r: e.activation(
                    hlb, hlb, AF.Identity,
                    scale=sb_wscale[:, 2 * NIB + r:2 * NIB + r + 1]),
                    waits=(ews1,))
                for sb8 in range(3):
                    n8 = min(8, NIB - 8 * sb8)
                    def ewmm8(e, hlb=hlb, sb8=sb8, n8=n8):
                        rr = None
                        for k8 in range(n8):
                            ib = 8 * sb8 + k8
                            rr = e.matmul(ps_b[:, 1024 + 128 * k8:1024 + 128 * (k8 + 1)],
                                          hlb[:, 128 * ib:128 * (ib + 1)],
                                          sb_i128b[:, :], start=True, stop=True)
                        return rr
                    mt8 = G("tensor", ewmm8, waits=(ewst[p], ew_ev) +
                            ((e15, g2) if r == 0 and sb8 == 0 else ()))
                    def ewev8(e, r=r, sb8=sb8, n8=n8):
                        return e.tensor_copy(
                            bass.AP(tensor=sb_wdT, offset=(8 * sb8) * DM + r * 128,
                                    ap=[[NIB * DM, 128], [DM, n8], [1, 128]]),
                            bass.AP(tensor=ps_b, offset=1024,
                                    ap=[[2048, 128], [128, n8], [1, 128]]))
                    ew_ev = G("vector", ewev8, waits=(mt8, e16))
            wd_done = (ew_ev, ewst[0], ewst[1], ewig)

            def issue_loads(c, buf, waits):
                k = "ldd" + str(buf)
                fns = []
                for par in range(2):
                    fns.append(lambda e, c=c, buf=buf, par=par: e.dma_start(
                        out=bass.AP(tensor=sb_xTc, offset=buf * NH * 128 + par * 128,
                                    ap=[[2 * NH * 128, 64], [256, NKC], [1, 128]]),
                        in_=bass.AP(tensor=xactT_sp.tensor,
                                    offset=c * 128 * NKC * 128 + par * 64 * NKC * 128,
                                    ap=[[NKC * 128, 64], [128, NKC], [1, 128]])))
                fns.append(lambda e, c=c, buf=buf: e.dma_start(
                    out=sb_xkc2[:, buf * DM:(buf + 1) * DM], in_=xacttok_sp[c]))
                fns.append(lambda e, c=c, buf=buf: e.dma_start(
                    out=sb_stc[:, buf * DM:(buf + 1) * DM], in_=static_sp[c]))
                fns.append(lambda e, c=c, buf=buf: e.dma_start(
                    out=sb_prc[:, buf * NH * DH:(buf + 1) * NH * DH], in_=pr_sp[c]))
                fns.append(lambda e, c=c, buf=buf: e.dma_start(
                    out=sb_rtc[:, buf * NH * 128:(buf + 1) * NH * 128], in_=rt_sp[c]))
                return GD("sync", fns, waits=waits, key=k)

            def hload(c, p, waits):
                def f(e, c=c, p=p):
                    return e.dma_start(
                        out=sb_hc[:, IFF * p:IFF * (p + 1)],
                        in_=bass.AP(tensor=h_sp.tensor, offset=c * 128,
                                    ap=[[T, 128], [128 * T, NIB], [1, 128]]))
                return GD("gpsimd", [f], waits=waits, key="hst" + str(p))

            ld5 = [None, None]
            ld5[0] = issue_loads(0, 0, pc)
            NTE = NT + 4
            ht_t = [None] * NT
            esh_t = [None] * NT
            ecl_t = [None] * NT
            ecs_t = [None] * NT
            eig_t = [None] * NT
            est_t = [None] * NT
            dmm_t = [None] * NT
            ff_t = [None] * NT
            o1_t = [None] * NT
            o2_t = [None] * NT
            ost_t = [None] * NT
            xt_t = [None] * NT
            attnw_t = [None] * NT
            ht_t[0] = hload(0, 0, pc)
            if NT > 1:
                ht_t[1] = hload(1, 1, pc)
            xt_t[0] = GD("sync", [lambda e: e.dma_start(out=sb_xr[:, :], in_=x_in[0:128, :])],
                         waits=pc, key="ext0")

            gr2a = gr2b = u_ev = ks_ev = None
            y_t = [None, None]
            su_ev = s_ev
            dk2_t = None
            attn_prev = None
            kst = smt = None
            tok = lambda a, j: (a[j] if 0 <= j < NT else None)
            for i in range(NTE):
                j = i - 4
                if 0 <= j < NT:
                    ff_t[j] = G("scalar", lambda e: e.activation(
                        sb_scr2[:, :], ps_b[:, 1024:2048], AF.Identity,
                        scale=col("scaleh")),
                        waits=(dmm_t[j], e16, tok(o2_t, j - 1)))
                    o1_t[j] = G("vector", lambda e, j=j: e.scalar_tensor_tensor(
                        sb_outb[:, :], sb_xr[:, :], 0.0,
                        sb_attn[:, DM * (j % 6):DM * (j % 6 + 1)], ALU.add, ALU.add),
                        waits=(xt_t[j], attnw_t[j], tok(ost_t, j - 1)))
                    if j + 1 < NT:
                        xt_t[j + 1] = GD("sync", [lambda e, j=j: e.dma_start(
                            out=sb_xr[:, :], in_=x_in[128 * (j + 1):128 * (j + 2), :])],
                            waits=(o1_t[j],), key="ext" + str(j % 2))
                    o2_t[j] = G("gpsimd", lambda e: e.tensor_tensor(
                        sb_outb[:, :], sb_outb[:, :], sb_scr2[:, :], ALU.add),
                        waits=(o1_t[j], ff_t[j]))
                    ost_t[j] = GD("gpsimd", [lambda e, j=j: e.dma_start(
                        out=out_t[128 * j:128 * (j + 1), :], in_=sb_outb[:, :])],
                        waits=(o2_t[j],), key="sto" + str(j % 2))
                j = i - 3
                if 0 <= j < NT:
                    p = j % 2
                    hlb = sb_hlev[:, IFF * p:IFF * (p + 1)]
                    def dmms(e, hlb=hlb):
                        r = None
                        for half in range(2):
                            for ib in range(NIB):
                                e.matmul(ps_b[:, 1024 + 512 * half:1024 + 512 * (half + 1)],
                                         hlb[:, 128 * ib:128 * (ib + 1)],
                                         sb_wdT[:, ib * DM + 512 * half:
                                                ib * DM + 512 * (half + 1)],
                                         start=(ib == 0), stop=False)
                            r = e.matmul(ps_b[:, 1024 + 512 * half:1024 + 512 * (half + 1)],
                                         sb_onesb[0:1, :],
                                         sb_brow[0:1, 512 * half:512 * (half + 1)],
                                         start=False, stop=True)
                        return r
                    dmm_t[j] = G("tensor", dmms, waits=(est_t[j], tok(ff_t, j - 1), eb2) +
                                 (wd_done + (e15,) if j < 2 else ()))
                j = i - 2
                if 0 <= j < NT:
                    p = j % 2
                    hub = sb_hu[:, IFF * p:IFF * (p + 1)]
                    hlb = sb_hlev[:, IFF * p:IFF * (p + 1)]
                    eig_t[j] = G("vector", lambda e, hub=hub, hlb=hlb: e.tensor_tensor(
                        hlb, sb_hint[:, :], hub, ALU.is_gt),
                        waits=(ecs_t[j], tok(est_t, j - 1), tok(dmm_t, j - 2)) +
                        (wd_done if j < 2 else ()))
                    est_t[j] = G("vector", lambda e, hlb=hlb: e.scalar_tensor_tensor(
                        hlb, sb_hint[:, :], 128.0, hlb, ALU.subtract, ALU.subtract),
                        waits=(eig_t[j],))
                j = i - 1
                if 0 <= j < NT:
                    p = j % 2
                    hub = sb_hu[:, IFF * p:IFF * (p + 1)]
                    ecl_t[j] = G("gpsimd", lambda e, hub=hub: e.tensor_scalar(
                        hub, hub, 255.99, 0.01, ALU.min, ALU.max),
                        waits=(esh_t[j], tok(ecs_t, j - 1)))
                    ecs_t[j] = G("scalar", lambda e, hub=hub: e.activation(
                        sb_hint[:, :], hub, AF.Identity),
                        waits=(ecl_t[j], tok(est_t, j - 1)) + (wd_done if j == 0 else ()))
                j = i
                if j < NT:
                    p = j % 2
                    hub = sb_hu[:, IFF * p:IFF * (p + 1)]
                    esh_t[j] = G("scalar", lambda e, p=p, hub=hub: e.activation(
                        hub, sb_hc[:, IFF * p:IFF * (p + 1)], AF.Identity,
                        bias=col("c1285"), scale=col("invh")),
                        waits=(ht_t[j], e14, tok(eig_t, j - 2)) + (wd_done if j < 2 else ()))
                    if j + 2 < NT:
                        ht_t[j + 2] = hload(j + 2, j % 2, (esh_t[j],))

                c = i
                if c >= NT:
                    continue
                buf = c % 2
                yb = sb_y[:, DM * buf:DM * (buf + 1)]
                if c + 1 < NT:
                    ld5[(c + 1) % 2] = issue_loads(
                        c + 1, (c + 1) % 2,
                        (gr2a, gr2b, u_ev, dk2_t, y_t[(c + 1) % 2])
                        if c >= 1 else (pc + (s_ev,)))
                xTc = sb_xTc[:, buf * NH * 128:(buf + 1) * NH * 128]
                xkc = sb_xkc2[:, buf * DM:(buf + 1) * DM]
                stc = sb_stc[:, buf * DM:(buf + 1) * DM]
                def gmm2(e, xTc=xTc):
                    r = None
                    for h in range(NH):
                        sl = xTc[:, 128 * h:128 * (h + 1)]
                        r = e.matmul(ps_a[:, 128 * h:128 * (h + 1)], sl, sl,
                                     start=True, stop=True)
                    return r
                grt = G("tensor", gmm2, waits=(ld5[buf], gr2a, gr2b, y_t[0], y_t[1], su_ev)
                        + ((g3,) if c == 0 else ()))
                gr2a = G("vector", lambda e: e.tensor_tensor(
                    sb_aot[:, :], ps_a[:, 0:2048], sb_mA[:, :], ALU.mult),
                    waits=(grt,))
                gr2b = gr2a
                def umms(e, buf=buf):
                    r = None
                    for h in range(NH):
                        r = e.matmul(ps_b[:, 64 * h:64 * (h + 1)],
                                     sb_rtc[:, buf * NH * 128 + 128 * h:
                                            buf * NH * 128 + 128 * (h + 1)],
                                     sb_S[:, 64 * h:64 * (h + 1)],
                                     start=True, stop=True)
                    return r
                umt = G("tensor", umms, waits=(ld5[buf], su_ev, u_ev, ks_ev) +
                        ((e14, e16) if c == 0 else ()))
                u_ev = G("vector", lambda e, buf=buf: e.scalar_tensor_tensor(
                    v3(sb_U, 0, NH * DH, NH, 64, 64),
                    v3(ps_b, 0, 2048, NH, 64, 64), -1.0,
                    bass.AP(tensor=sb_prc, offset=buf * NH * DH,
                            ap=[[2 * NH * DH, 128], [64, NH], [1, 64]]),
                    ALU.mult, ALU.add), waits=(umt,))
                def kmms(e, xTc=xTc):
                    r = None
                    for h in range(NH):
                        r = e.matmul(ps_b[:, 64 * h:64 * (h + 1)],
                                     xTc[:, 128 * h:128 * (h + 1)],
                                     sb_S[:, 64 * h:64 * (h + 1)],
                                     start=True, stop=True)
                    return r
                kst = G("tensor", kmms, waits=(u_ev,))
                ks_ev = G("vector", lambda e, stc=stc: e.scalar_tensor_tensor(
                    sb_ks[:, :], ps_b[:, 0:1024], col("lam_t"), stc, ALU.mult, ALU.add),
                    waits=(kst, ld5[buf]))
                def omms(e):
                    r = None
                    for h in range(NH):
                        r = e.matmul(ps_a[:, 64 * h:64 * (h + 1)],
                                     sb_aot[:, 128 * h:128 * (h + 1)],
                                     sb_U[:, 64 * h:64 * (h + 1)],
                                     start=True, stop=True)
                    return r
                omt = G("tensor", omms, waits=(u_ev, gr2a, gr2b))
                dk2_t = G("scalar", lambda e, xkc=xkc: e.activation(
                    sb_dk2[:, :], xkc, AF.Identity, scale=col("lam_ct")),
                    waits=(ld5[buf], dk2_t, smt))
                def smms(e):
                    r = None
                    for h in range(NH):
                        r = e.matmul(ps_a[0:64, 1024 + 64 * h:1024 + 64 * (h + 1)],
                                     sb_dk2[:, 64 * h:64 * (h + 1)],
                                     sb_U[:, 64 * h:64 * (h + 1)],
                                     start=True, stop=True)
                    return r
                smt = G("tensor", smms, waits=(dk2_t, u_ev))
                su_ev = G("vector", lambda e: e.scalar_tensor_tensor(
                    sb_S[:, :], sb_S[:, :], sb_cols[0:64, COL["lamC"]:COL["lamC"] + 1],
                    ps_a[0:64, 1024:2048], ALU.mult, ALU.add), waits=(smt,))
                y_t[buf] = G("vector", lambda e, yb=yb: e.tensor_tensor(
                    yb, ps_a[:, 0:1024], sb_ks[:, :], ALU.add), waits=(omt, ks_ev))
                if debug_outs and c == 0:
                    y_t[buf] = GD("gpsimd", [lambda e, yb=yb: e.dma_start(out=dbg["d_y"],
                                                                          in_=yb)],
                                  waits=(y_t[buf],), key="dbgs")
                sc8 = 8 * buf
                l1 = G("scalar", lambda e, yb=yb, sc8=sc8: e.activation(
                    sb_junkDE[:, :], yb, AF.Square, accum_out=sb_stats[:, sc8:sc8 + 1]),
                    waits=(y_t[buf],))
                l2 = G("vector", lambda e, yb=yb, sc8=sc8: e.reduce_sum(
                    sb_stats[:, sc8 + 1:sc8 + 2], yb, axis=AX.X), waits=(y_t[buf],))
                l3 = G("gpsimd", lambda e, sc8=sc8: e.tensor_scalar_mul(
                    sb_stats[:, sc8 + 2:sc8 + 3], sb_stats[:, sc8 + 1:sc8 + 2], 1.0 / DM),
                    waits=(l2,))
                l4 = G("gpsimd", lambda e, sc8=sc8: e.tensor_tensor(
                    sb_stats[:, sc8 + 3:sc8 + 4], sb_stats[:, sc8 + 2:sc8 + 3],
                    sb_stats[:, sc8 + 2:sc8 + 3], ALU.mult), waits=(l3,))
                l5 = G("vector", lambda e, sc8=sc8: e.scalar_tensor_tensor(
                    sb_stats[:, sc8 + 4:sc8 + 5], sb_stats[:, sc8 + 0:sc8 + 1],
                    1.0 / DM, sb_stats[:, sc8 + 3:sc8 + 4], ALU.mult, ALU.subtract),
                    waits=(l4, l1))
                l6 = G("gpsimd", lambda e, sc8=sc8: e.tensor_scalar_add(
                    sb_stats[:, sc8 + 4:sc8 + 5], sb_stats[:, sc8 + 4:sc8 + 5], 1e-5),
                    waits=(l5,))
                l7 = G("scalar", lambda e, sc8=sc8: e.activation(
                    sb_stats[:, sc8 + 5:sc8 + 6], sb_stats[:, sc8 + 4:sc8 + 5], AF.Sqrt),
                    waits=(l6,))
                l8 = G("vector", lambda e, sc8=sc8: e.reciprocal(
                    sb_stats[:, sc8 + 6:sc8 + 7], sb_stats[:, sc8 + 5:sc8 + 6]), waits=(l7,))
                l9 = G("gpsimd", lambda e, sc8=sc8: e.tensor_tensor(
                    sb_stats[:, sc8 + 7:sc8 + 8], sb_stats[:, sc8 + 6:sc8 + 7],
                    sb_stats[:, sc8 + 6:sc8 + 7], ALU.mult), waits=(l8,))
                l10 = G("gpsimd", lambda e, sc8=sc8: e.tensor_tensor(
                    sb_stats[:, sc8 + 7:sc8 + 8], sb_stats[:, sc8 + 7:sc8 + 8],
                    sb_stats[:, sc8 + 4:sc8 + 5], ALU.mult), waits=(l9,))
                l11 = G("gpsimd", lambda e, sc8=sc8: e.tensor_scalar(
                    sb_stats[:, sc8 + 7:sc8 + 8], sb_stats[:, sc8 + 7:sc8 + 8],
                    -0.5, 1.5, ALU.mult, ALU.add), waits=(l10,))
                l12 = G("gpsimd", lambda e, sc8=sc8: e.tensor_tensor(
                    sb_stats[:, sc8 + 6:sc8 + 7], sb_stats[:, sc8 + 6:sc8 + 7],
                    sb_stats[:, sc8 + 7:sc8 + 8], ALU.mult), waits=(l11,))
                lapply = G("vector", lambda e, yb=yb, sc8=sc8: e.tensor_scalar(
                    sb_scr[:, :], yb, sb_stats[:, sc8 + 2:sc8 + 3],
                    sb_stats[:, sc8 + 6:sc8 + 7], ALU.subtract, ALU.mult),
                    waits=(l12, attn_prev))
                attnw_t[c] = G("gpsimd", lambda e, c=c: e.tensor_tensor(
                    sb_attn[:, DM * (c % 6):DM * (c % 6 + 1)], sb_scr[:, :], sb_gb[:, :],
                    ALU.mult), waits=(lapply, tok(o1_t, c - 6)) + ((g3,) if c == 0 else ()))
                attn_prev = attnw_t[c]
            G("gpsimd", lambda e: e.nop(), waits=(ost_t[NT - 1],))

        with nc.Block() as block:
            sc.finalize(block)
    return nc


_NC_CACHE = {}

def _get_nc(T, n_cores, debug_outs=False):
    key = (T, n_cores, debug_outs)
    if key not in _NC_CACHE:
        _NC_CACHE[key] = build_nc(T, n_cores, debug_outs)
    return _NC_CACHE[key]


def make_host_constants():
    j = np.arange(128)
    tm, sm = np.meshgrid(j, j, indexing="ij")
    diffT = (tm - sm).T
    pos = np.where(diffT > 0, diffT, 0)
    ind = np.zeros((7, 128, 128), dtype=np.float32)
    for b in range(7):
        ind[b] = ((pos >> b) & 1).astype(np.float32)
    return {
        "ind_bits": ind,
        "strictT": (diffT > 0).astype(np.float32),
        "inclT": (diffT >= 0).astype(np.float32),
        "i128_bf16": np.eye(128, dtype=BFNP),
        "i128_f32": np.eye(128, dtype=np.float32),
        "i64_f32": np.eye(64, dtype=np.float32),
        "ones_row": np.ones((1, 128), dtype=np.float32),
    }


def make_in_maps(inputs, n_cores=8):
    x = np.ascontiguousarray(np.asarray(inputs["L0_attn_x"], dtype=np.float32))
    B, L, Dm = x.shape
    Tall = B * L
    Tc = Tall // n_cores
    cores_per_batch = max(1, n_cores // B)
    xf = x.reshape(Tall, Dm)
    consts = make_host_constants()
    common = dict(
        W=np.ascontiguousarray(inputs["L0_attn_W"], dtype=np.float32),
        wg=np.ascontiguousarray(inputs["L0_ffn_w_gate"], dtype=np.float32),
        wu=np.ascontiguousarray(inputs["L0_ffn_w_up"], dtype=np.float32),
        wd=np.ascontiguousarray(inputs["L0_ffn_w_down"], dtype=np.float32),
        ln_g=np.ascontiguousarray(np.asarray(inputs["L0_attn_ln_gamma"],
                                             dtype=np.float32).reshape(1, Dm)),
        ln_b=np.ascontiguousarray(np.asarray(inputs["L0_attn_ln_beta"],
                                             dtype=np.float32).reshape(1, Dm)),
        eta=np.asarray(inputs["L0_attn_eta"], dtype=np.float32).reshape(1, 1),
        lam_logit=np.asarray(inputs["L0_attn_lam_logit"], dtype=np.float32).reshape(1, 1),
        clip_x=np.asarray(inputs["L0_ffn_clip_x"], dtype=np.float32).reshape(1, 1),
        clip_down=np.asarray(inputs["L0_ffn_clip_down"], dtype=np.float32).reshape(1, 1),
        **consts,
    )
    in_maps = []
    for i in range(n_cores):
        sel = np.zeros((64, n_cores), dtype=np.float32)
        if i % cores_per_batch != 0:
            sel[:, i - 1] = 1.0
        m = dict(common)
        m["x"] = np.ascontiguousarray(xf[i * Tc:(i + 1) * Tc])
        m["sel"] = sel
        in_maps.append(m)
    return in_maps, Tc


def kernel(**inputs):
    x = np.asarray(inputs["L0_attn_x"])
    B, L, Dm = x.shape
    n_cores = 8
    in_maps, Tc = make_in_maps(inputs, n_cores)
    nc = _get_nc(Tc, n_cores)
    res = run_bass_kernel_spmd(nc, in_maps, core_ids=list(range(n_cores)))
    out = np.concatenate([np.asarray(r["out"]) for r in res.results], axis=0)
    return out.reshape(B, L, Dm).astype(np.float32)


# version 36
# speedup vs baseline: 1.0086x; 1.0086x over previous
import numpy as np
import ml_dtypes
from contextlib import ExitStack

import concourse.bass as bass
import concourse.mybir as mybir
from concourse.bass_utils import run_bass_kernel_spmd

F32 = mybir.dt.float32
BF16 = mybir.dt.bfloat16
I32 = mybir.dt.int32
ALU = mybir.AluOpType
AF = mybir.ActivationFunctionType
AX = mybir.AxisListType
BFNP = ml_dtypes.bfloat16

DM = 1024
NH = 16
DH = 64
IFF = 2816
NKC = DM // 128
NIB = IFF // 128
NEU = 3


class Sched:

    ENGS = ("tensor", "vector", "scalar", "gpsimd", "sync")
    STREAMS = ("dma_s", "dma_g", "cc1", "cc2", "cc3",
               "lda0", "lda1", "spk0", "spk1",
               "ldk0", "ldk1", "spst", "sppr", "sprt",
               "ld20", "ld21", "hst0", "hst1", "spT0", "spT1", "wW0", "wW1",
               "ldd0", "ldd1", "eht0", "eht1", "ext0", "ext1",
               "wq0", "wq1",
               "sto0", "sto1", "dbgs")

    def __init__(self, nc, stack):
        self.nc = nc
        self.sem = {}
        self.cnt = {}
        for e in self.ENGS + self.STREAMS:
            self.sem[e] = stack.enter_context(nc.semaphore(f"s_{e}"))
            self.cnt[e] = 0
        self.lists = {e: [] for e in self.ENGS}
        self.sat = {e: {} for e in self.ENGS}

    def _waitlist(self, eng, waits):
        wl = []
        for w in waits:
            if w is None:
                continue
            s, v = w
            if self.sat[eng].get(s, 0) >= v:
                continue
            wl.append((s, v))
            self.sat[eng][s] = v
        return tuple(wl)

    def g(self, eng, fn, waits=(), key=None, inc=1):
        key = key or eng
        self.cnt[key] += inc
        tgt = self.cnt[key]
        wl = self._waitlist(eng, waits)

        def closure(e, fn=fn, wl=wl, key=key, tgt=tgt, inc=inc):
            for s, v in wl:
                e.wait_ge(self.sem[s], v)
            last = fn(e)
            last.then_inc(self.sem[key], inc)
        self.lists[eng].append(closure)
        return (key, tgt)

    def gdma(self, eng, fns, waits=(), key=None):
        key = key or ("dma_s" if eng == "sync" else "dma_g")
        prev = (key, self.cnt[key]) if self.cnt[key] > 0 else None
        self.cnt[key] += 16 * len(fns)
        tgt = self.cnt[key]
        wl = self._waitlist(eng, tuple(waits) + (prev,))

        def closure(e, fns=tuple(fns), wl=wl, key=key):
            for s, v in wl:
                e.wait_ge(self.sem[s], v)
            for f in fns:
                f(e).then_inc(self.sem[key], 16)
        self.lists[eng].append(closure)
        return (key, tgt)

    def barrier(self):
        return tuple((e, self.cnt[e]) for e in self.ENGS + self.STREAMS
                     if self.cnt[e] > 0)

    def finalize(self, block):
        for name in self.ENGS:
            lst = self.lists[name]

            def body(e, lst=lst):
                for c in lst:
                    c(e)
            getattr(block, name)(body)


def build_nc(T, n_cores=8, debug_outs=False, stop_after=None):
    NT = T // 128
    nc = bass.Bass("TRN2", target_bir_lowering=False, debug=False)

    x_in = nc.dram_tensor("x", [T, DM], F32, kind="ExternalInput").ap()
    W_in = nc.dram_tensor("W", [DM, DM], F32, kind="ExternalInput").ap()
    wg_in = nc.dram_tensor("wg", [IFF, DM], F32, kind="ExternalInput").ap()
    wu_in = nc.dram_tensor("wu", [IFF, DM], F32, kind="ExternalInput").ap()
    wd_in = nc.dram_tensor("wd", [DM, IFF], F32, kind="ExternalInput").ap()
    lng_in = nc.dram_tensor("ln_g", [1, DM], F32, kind="ExternalInput").ap()
    lnb_in = nc.dram_tensor("ln_b", [1, DM], F32, kind="ExternalInput").ap()
    eta_in = nc.dram_tensor("eta", [1, 1], F32, kind="ExternalInput").ap()
    lml_in = nc.dram_tensor("lam_logit", [1, 1], F32, kind="ExternalInput").ap()
    cx_in = nc.dram_tensor("clip_x", [1, 1], F32, kind="ExternalInput").ap()
    cd_in = nc.dram_tensor("clip_down", [1, 1], F32, kind="ExternalInput").ap()
    sel_in = nc.dram_tensor("sel", [64, n_cores], F32, kind="ExternalInput").ap()
    ind_in = nc.dram_tensor("ind_bits", [7, 128, 128], F32, kind="ExternalInput").ap()
    strict_in = nc.dram_tensor("strictT", [128, 128], F32, kind="ExternalInput").ap()
    incl_in = nc.dram_tensor("inclT", [128, 128], F32, kind="ExternalInput").ap()
    i128b_in = nc.dram_tensor("i128_bf16", [128, 128], BF16, kind="ExternalInput").ap()
    i128f_in = nc.dram_tensor("i128_f32", [128, 128], F32, kind="ExternalInput").ap()
    i64f_in = nc.dram_tensor("i64_f32", [64, 64], F32, kind="ExternalInput").ap()
    ones_in = nc.dram_tensor("ones_row", [1, 128], F32, kind="ExternalInput").ap()

    out_t = nc.dram_tensor("out", [T, DM], F32, kind="ExternalOutput").ap()

    xactT_sp = nc.dram_tensor("xactT_sp", [NT, 128, NKC, 128], BF16).ap()
    xacttok_sp = nc.dram_tensor("xacttok_sp", [NT, 128, DM], BF16).ap()
    static_sp = nc.dram_tensor("static_sp", [NT, 128, DM], BF16).ap()
    pr_sp = nc.dram_tensor("pr_sp", [NT, 128, NH * DH], BF16).ap()
    rt_sp = nc.dram_tensor("rt_sp", [NT, 64, NH * 128], BF16).ap()
    h_sp = nc.dram_tensor("h_sp", [IFF, T], BF16).ap()
    dp_sp = nc.dram_tensor("dp_sp", [520], F32).ap()
    row_sp = nc.dram_tensor("row_sp", [1024], F32).ap()
    wlev_sp = nc.dram_tensor("wlev_sp", [2 * NIB, 128, DM], BF16).ap()
    cb1_i = nc.dram_tensor("cb1_i", [16], F32).ap()
    cb1_o = nc.dram_tensor("cb1_o", [16], F32).ap()
    cb2_i = nc.dram_tensor("cb2_i", [16], F32).ap()
    cb2_o = nc.dram_tensor("cb2_o", [16], F32).ap()
    cb3_i = nc.dram_tensor("cb3_i", [64 * NH * DH], F32).ap()
    cb3_o = nc.dram_tensor("cb3_o", [n_cores, 64 * NH * DH], F32).ap()

    dbg = {}
    if debug_outs:
        for nm, shp in [("d_xn", [128, DM]), ("d_static", [128, DM]),
                        ("d_gmax", [1, 1]), ("d_hmax", [1, 1]),
                        ("d_xlev", [128, DM]), ("d_y", [128, DM]),
                        ("d_sstart", [64, NH * DH])]:
            dbg[nm] = nc.dram_tensor(nm, shp, F32, kind="ExternalOutput").ap()

    stack = ExitStack()
    with stack:
        sc = Sched(nc, stack)
        G = sc.g
        GD = sc.gdma
        dp_stack = ExitStack()

        def alloc(stk, name, shape, dtype):
            return stk.enter_context(nc.sbuf_tensor(name, shape, dtype))

        sb_tiny = alloc(stack, "tiny", [1, 64], F32)
        sb_cols = alloc(stack, "cols", [128, 24], F32)
        sb_stats = alloc(stack, "stats", [128, max(2 * NT, 16)], F32)
        sb_mD = alloc(stack, "maskD", [128, NH * 128], BF16)
        sb_mA = alloc(stack, "maskA", [128, NH * 128], BF16)
        sb_i128b = alloc(stack, "i128b", [128, 128], BF16)
        sb_i128f = alloc(stack, "i128f", [128, 128], F32)
        sb_ks = alloc(stack, "kstile", [128, NH * DH], F32)
        sb_ones = alloc(stack, "onesrow", [1, 128], F32)
        sb_onesb = alloc(stack, "onesrowb", [1, 128], BF16)
        sb_scr = alloc(stack, "scr", [128, 1024], F32)
        sb_scr2 = alloc(stack, "scr2", [128, 1024], F32)
        sb_wscale = alloc(stack, "wscale", [128, 2 * NIB + 8], F32)
        sb_S0 = alloc(stack, "S0_tile", [64, NH * DH], BF16)
        sb_sel = alloc(stack, "selrep", [64, n_cores], F32)
        sb_dp = dp_stack.enter_context(nc.sbuf_tensor("dp", [1, 520], F32))

        ps_a = stack.enter_context(nc.psum_tensor("PA", [128, 2048], F32))
        ps_b = stack.enter_context(nc.psum_tensor("PB", [128, 2048], F32))

        COL = {"beta": 0, "nbeta": 1, "lamC": 2, "scalex": 3, "invx": 4,
               "scaleh": 5, "invh": 6, "lam_t": 7, "lam_ct": 8, "blam_t": 9,
               "q0": 10, "c1285": 17}
        def col(name, off=0):
            c = COL[name] + off
            return sb_cols[:, c:c + 1]

        TI = {"lam": 0, "one_m": 1, "beta": 2, "sigcx": 3, "sigcd": 4,
              "gmax": 5, "hmax": 6, "scalex": 7, "invx": 8, "scaleh": 9,
              "invh": 10, "t1": 11, "t2": 12, "pad": 16}
        def tiny(name, n=1):
            i = TI[name]
            return sb_tiny[0:1, i:i + n]

        def v3(tensor, offset, pstride, mid_n, mid_stride, inner_n, np_=128):
            return bass.AP(tensor=tensor, offset=offset,
                           ap=[[pstride, np_], [mid_stride, mid_n], [1, inner_n]])

        t = [None]
        def S(eng, fn, extra=(), key=None, inc=1):
            t[0] = G(eng, fn, waits=(t[0],) + tuple(extra), key=key, inc=inc)
            return t[0]
        def SD(eng, fns, extra=(), key=None):
            t[0] = GD(eng, fns, waits=(t[0],) + tuple(extra), key=key)
            return t[0]

        S("vector", lambda e: e.memset(sb_cols[:, COL["c1285"]:COL["c1285"] + 1], 128.5))
        SD("sync", [lambda e: e.dma_start(out=sb_i128b[:, :], in_=i128b_in),
                    lambda e: e.dma_start(out=sb_i128f[:, :], in_=i128f_in),
                    lambda e: e.dma_start(out=sb_ones[:, :], in_=ones_in),
                    lambda e: e.dma_start(out=sb_sel[:, :], in_=sel_in),
                    lambda e: e.dma_start(out=tiny("lam"), in_=lml_in),
                    lambda e: e.dma_start(out=tiny("one_m"), in_=eta_in),
                    lambda e: e.dma_start(out=tiny("sigcx"), in_=cx_in),
                    lambda e: e.dma_start(out=tiny("sigcd"), in_=cd_in)])
        S("scalar", lambda e: e.activation(tiny("lam"), tiny("lam"), AF.Sigmoid))
        S("vector", lambda e: e.tensor_scalar(tiny("t1"), tiny("lam"), -1.0, 1.0, ALU.mult, ALU.add))
        S("vector", lambda e: e.tensor_tensor(tiny("t2"), tiny("t1"), tiny("one_m"), ALU.mult))
        S("vector", lambda e: e.tensor_scalar_mul(tiny("beta"), tiny("t2"), 1.0 / DH))
        S("scalar", lambda e: e.activation(tiny("sigcx"), tiny("sigcx"), AF.Sigmoid))
        S("scalar", lambda e: e.activation(tiny("sigcd"), tiny("sigcd"), AF.Sigmoid))
        S("vector", lambda e: e.memset(sb_dp[0:1, :], 0.0))
        S("vector", lambda e: e.memset(sb_dp[0:1, 0:1], 1.0))
        S("vector", lambda e: e.tensor_copy(sb_dp[0:1, 1:2], tiny("lam")))
        n_have = 2
        while n_have < 257:
            n_new = min(n_have - 1, 257 - n_have)
            def dbl(e, n_have=n_have, n_new=n_new):
                return e.tensor_scalar(sb_dp[0:1, n_have:n_have + n_new],
                                       sb_dp[0:1, 1:1 + n_new],
                                       sb_dp[0:1, n_have - 1:n_have], None, ALU.mult)
            S("vector", dbl)
            n_have += n_new
        S("vector", lambda e: e.tensor_copy(
            sb_dp[0:1, 260:260 + 257],
            bass.AP(tensor=sb_dp, offset=256, ap=[[520, 1], [-1, 257]])))
        SD("gpsimd", [lambda e: e.dma_start(out=dp_sp[0:520], in_=sb_dp[0:1, 0:520])])
        SD("sync", [lambda e: e.dma_start(
                        out=col("lam_t"),
                        in_=bass.AP(tensor=dp_sp.tensor, offset=1, ap=[[1, 128], [1, 1]])),
                    lambda e: e.dma_start(
                        out=col("lam_ct"),
                        in_=bass.AP(tensor=dp_sp.tensor, offset=260 + 129, ap=[[1, 128], [1, 1]]))])

        def emit_rep(src_ap, dst_col, scale_mult=None, add_const=None):
            S("tensor", lambda e: e.matmul(ps_a[:, 0:1], sb_ones[0:1, :], src_ap,
                                           start=True, stop=True))
            if scale_mult is None and add_const is None:
                S("vector", lambda e: e.tensor_copy(dst_col, ps_a[:, 0:1]))
            else:
                sm = 1.0 if scale_mult is None else scale_mult
                ac = 0.0 if add_const is None else add_const
                S("vector", lambda e: e.tensor_scalar(dst_col, ps_a[:, 0:1], sm, ac,
                                                      ALU.mult, ALU.add))

        emit_rep(tiny("beta"), col("beta"))
        emit_rep(tiny("beta"), col("nbeta"), scale_mult=-1.0)
        emit_rep(sb_dp[0:1, 128:129], col("lamC"))
        for b in range(7):
            emit_rep(sb_dp[0:1, (1 << b):(1 << b) + 1], col("q0", b), add_const=-1.0)
        S("vector", lambda e: e.tensor_tensor(col("blam_t"), col("lam_t"), col("beta"), ALU.mult))
        S("vector", lambda e: e.tensor_copy(sb_onesb[:, :], sb_ones[:, :]))
        SD("sync", [lambda e: e.dma_start(
            out=sb_scr2[:, 0:896],
            in_=bass.AP(tensor=ind_in.tensor, offset=0,
                        ap=[[128, 128], [128 * 128, 7], [1, 128]]))])
        S("vector", lambda e: e.memset(sb_scr[:, 0:128], 1.0))
        for b in range(7):
            def qb(e, b=b):
                return e.tensor_scalar(sb_scr[:, 128:256], sb_scr2[:, 128 * b:128 * (b + 1)],
                                       col("q0", b), 1.0, ALU.mult, ALU.add)
            S("vector", qb)
            S("vector", lambda e: e.tensor_tensor(sb_scr[:, 0:128], sb_scr[:, 0:128],
                                                  sb_scr[:, 128:256], ALU.mult))
        SD("sync", [lambda e: e.dma_start(out=sb_scr[:, 256:384], in_=strict_in),
                    lambda e: e.dma_start(out=sb_scr[:, 384:512], in_=incl_in)])
        S("vector", lambda e: e.tensor_tensor(sb_scr[:, 512:640], sb_scr[:, 0:128],
                                              sb_scr[:, 256:384], ALU.mult))
        S("vector", lambda e: e.tensor_scalar(sb_mD[:, 0:128], sb_scr[:, 512:640],
                                              col("nbeta"), None, ALU.mult))
        S("vector", lambda e: e.tensor_tensor(sb_scr[:, 512:640], sb_scr[:, 0:128],
                                              sb_scr[:, 384:512], ALU.mult))
        S("vector", lambda e: e.tensor_copy(sb_mA[:, 0:128], sb_scr[:, 512:640]))
        for h in range(1, NH):
            S("vector", lambda e, h=h: e.tensor_copy(sb_mD[:, 128 * h:128 * (h + 1)], sb_mD[:, 0:128]))
            S("vector", lambda e, h=h: e.tensor_copy(sb_mA[:, 128 * h:128 * (h + 1)], sb_mA[:, 0:128]))
        setup_all = sc.barrier()
        dp_stack.close()

        if stop_after == "S":
            z1 = G("vector", lambda e: e.memset(sb_scr[:, :], 0.0), waits=setup_all)
            toks = []
            for c in range(NT):
                toks.append(GD("gpsimd", [lambda e, c=c: e.dma_start(
                    out=out_t[128 * c:128 * (c + 1), :], in_=sb_scr[:, :])],
                    waits=(z1,), key="sto0"))
            G("gpsimd", lambda e: e.nop(), waits=(toks[-1],))
            with nc.Block() as block:
                sc.finalize(block)
            return nc

        coll1_tok = coll3_tok = coll2_tok = None

        with ExitStack() as phAB:
            sb_xT = alloc(phAB, "xactT", [128, NKC * T], BF16)
            sb_WT = alloc(phAB, "WT", [128, NKC * DM], BF16)
            sb_xbuf = alloc(phAB, "xbuf", [128, 2 * DM], F32)
            sb_xtok = alloc(phAB, "xtokt", [128, 2 * DM], BF16)
            sb_xn = alloc(phAB, "xnscr", [128, 2 * DM], F32)
            sb_st8 = alloc(phAB, "statscr", [128, 32], F32)
            sb_max = alloc(phAB, "maxacc", [128, 2], F32)
            sb_wtb = alloc(phAB, "wtb", [128, 2 * DM], BF16)
            sb_junkA = alloc(phAB, "junkA", [128, 1024], BF16)
            sb_junkW = alloc(phAB, "junkW", [128, 2 * 1024], BF16)
            sb_wq = alloc(phAB, "wqstage", [128, 2 * DM], F32)
            sb_wlev = alloc(phAB, "wqlev", [128, 2 * DM], BF16)
            sb_wint = alloc(phAB, "wqint", [128, 2 * DM], I32)

            mx_t = G("vector", lambda e: e.memset(sb_max[:, :], 0.0), waits=setup_all)
            wql = [None, None]
            wqsh = [None, None]
            wqcs = [None, None]
            wqigp = [None, None]
            wqstp = [None, None]
            wqsp = [None, None]

            def emit_wq_back(wb):
                q = wb % 2
                wqb = sb_wq[:, DM * q:DM * (q + 1)]
                wlv = sb_wlev[:, DM * q:DM * (q + 1)]
                jw = sb_junkW[:, 1024 * q:1024 * (q + 1)]
                wib = sb_wint[:, DM * q:DM * (q + 1)]
                s8w = 24 + 4 * q
                wqig = G("vector", lambda e: e.tensor_tensor(
                    jw, wib, wqb, ALU.is_gt), waits=(wqcs[q],))
                wqigp[q] = wqig
                wqst = G("vector", lambda e: e.scalar_tensor_tensor(
                    wlv, wib, 128.0, jw, ALU.subtract, ALU.subtract),
                    waits=(wqig, wqsp[q]))
                wqstp[q] = wqst
                wqsc = G("scalar", lambda e: e.activation(
                    wlv, wlv, AF.Identity, scale=sb_st8[:, s8w + 3:s8w + 4]),
                    waits=(wqst,))
                wqsp[q] = GD("gpsimd", [lambda e, wb=wb: e.dma_start(
                    out=wlev_sp[wb], in_=wlv)], waits=(wqsc,), key="wq" + str(q))

            ld = [None, None]
            sq = [None, None]
            xn_t = [None, None]
            tanh_t = [None, None]
            tpose_t = [None, None]
            evT = [None, None]
            spk = [None, None]
            spT_t = [None, None]
            dbg_xn_tok = None
            ld[0] = GD("sync", [lambda e: e.dma_start(out=sb_xbuf[:, 0:DM], in_=x_in[0:128, :])],
                       waits=setup_all, key="lda0")
            for c in range(NT):
                p = c % 2
                xb = sb_xbuf[:, DM * p:DM * (p + 1)]
                xnb = sb_xn[:, DM * p:DM * (p + 1)]
                xtk = sb_xtok[:, DM * p:DM * (p + 1)]
                s8 = 16 * p
                if c + 1 < NT:
                    q = (c + 1) % 2
                    ld[q] = GD("sync", [lambda e, c=c, q=q: e.dma_start(
                        out=sb_xbuf[:, DM * q:DM * (q + 1)],
                        in_=x_in[128 * (c + 1):128 * (c + 2), :])],
                        waits=(xn_t[q], sq[q]), key="lda" + str(q))
                sq[p] = G("scalar", lambda e, xb=xb, s8=s8: e.activation(
                    sb_junkA[:, :], xb, AF.Square, accum_out=sb_st8[:, s8:s8 + 1]),
                    waits=(ld[p], tanh_t[p]))
                v1 = G("vector", lambda e, xb=xb, s8=s8: e.reduce_sum(
                    sb_st8[:, s8 + 1:s8 + 2], xb, axis=AX.X), waits=(ld[p], xn_t[p]))
                v2 = G("vector", lambda e, s8=s8, c=c: e.tensor_scalar_mul(
                    sb_stats[:, 2 * c:2 * c + 1], sb_st8[:, s8 + 1:s8 + 2], 1.0 / DM),
                    waits=(v1,))
                v3_ = G("vector", lambda e, s8=s8, c=c: e.tensor_tensor(
                    sb_st8[:, s8 + 2:s8 + 3], sb_stats[:, 2 * c:2 * c + 1],
                    sb_stats[:, 2 * c:2 * c + 1], ALU.mult), waits=(v2,))
                v4 = G("vector", lambda e, s8=s8: e.scalar_tensor_tensor(
                    sb_st8[:, s8 + 3:s8 + 4], sb_st8[:, s8 + 0:s8 + 1],
                    1.0 / DM, sb_st8[:, s8 + 2:s8 + 3], ALU.mult, ALU.subtract),
                    waits=(v3_, sq[p]))
                v5 = G("vector", lambda e, s8=s8: e.tensor_scalar_add(
                    sb_st8[:, s8 + 3:s8 + 4], sb_st8[:, s8 + 3:s8 + 4], 1e-5), waits=(v4,))
                a1 = G("scalar", lambda e, s8=s8: e.activation(
                    sb_st8[:, s8 + 4:s8 + 5], sb_st8[:, s8 + 3:s8 + 4], AF.Sqrt), waits=(v5,))
                v6 = G("vector", lambda e, s8=s8: e.reciprocal(
                    sb_st8[:, s8 + 5:s8 + 6], sb_st8[:, s8 + 4:s8 + 5]), waits=(a1,))
                v7 = G("vector", lambda e, s8=s8: e.tensor_tensor(
                    sb_st8[:, s8 + 6:s8 + 7], sb_st8[:, s8 + 5:s8 + 6],
                    sb_st8[:, s8 + 5:s8 + 6], ALU.mult), waits=(v6,))
                v8 = G("vector", lambda e, s8=s8: e.tensor_tensor(
                    sb_st8[:, s8 + 6:s8 + 7], sb_st8[:, s8 + 6:s8 + 7],
                    sb_st8[:, s8 + 3:s8 + 4], ALU.mult), waits=(v7,))
                v9 = G("vector", lambda e, s8=s8: e.tensor_scalar(
                    sb_st8[:, s8 + 6:s8 + 7], sb_st8[:, s8 + 6:s8 + 7],
                    -0.5, 1.5, ALU.mult, ALU.add), waits=(v8,))
                v10 = G("vector", lambda e, s8=s8, c=c: e.tensor_tensor(
                    sb_stats[:, 2 * c + 1:2 * c + 2], sb_st8[:, s8 + 5:s8 + 6],
                    sb_st8[:, s8 + 6:s8 + 7], ALU.mult), waits=(v9,))
                xn_t[p] = G("vector", lambda e, xb=xb, xnb=xnb, c=c: e.tensor_scalar(
                    xnb, xb, sb_stats[:, 2 * c:2 * c + 1], sb_stats[:, 2 * c + 1:2 * c + 2],
                    ALU.subtract, ALU.mult), waits=(v10, tanh_t[p], dbg_xn_tok))
                mx_t = G("vector", lambda e, xnb=xnb: e.reduce_max(
                    sb_max[:, 1:2], xnb, axis=AX.X, apply_absolute_value=True),
                    waits=(xn_t[p], mx_t))
                mx_t = G("gpsimd", lambda e: e.tensor_scalar(
                    sb_max[:, 0:1], sb_max[:, 1:2], sb_max[:, 0:1], None, ALU.max),
                    waits=(mx_t,))
                tanh_t[p] = G("scalar", lambda e, xnb=xnb, xtk=xtk: e.activation(
                    xtk, xnb, AF.Tanh), waits=(xn_t[p], spk[p], tpose_t[p]))
                if debug_outs and c == 0:
                    dbg_xn_tok = GD("gpsimd", [lambda e: e.dma_start(out=dbg["d_xn"], in_=sb_xn[:, 0:DM])],
                                    waits=(xn_t[p],), key="dbgs")
                spk[p] = GD("gpsimd", [lambda e, c=c, xtk=xtk: e.dma_start(
                    out=xacttok_sp[c], in_=xtk)], waits=(tanh_t[p],), key="spk" + str(p))
                def tmms(e, xtk=xtk, p=p):
                    r = None
                    for fc in range(NKC):
                        r = e.matmul(ps_a[:, 1024 * p + 128 * fc:1024 * p + 128 * (fc + 1)],
                                     xtk[:, 128 * fc:128 * (fc + 1)],
                                     sb_i128b[:, :], start=True, stop=True)
                    return r
                tpose_t[p] = G("tensor", tmms,
                               waits=(tanh_t[p], evT[p]) + (setup_all if c < 2 else ()))
                evT[p] = G("vector", lambda e, c=c, p=p: e.tensor_copy(
                    bass.AP(tensor=sb_xT, offset=c * 128,
                            ap=[[NKC * T, 128], [T, NKC], [1, 128]]),
                    ps_a[:, 1024 * p:1024 * p + 1024]), waits=(tpose_t[p], spT_t[p]))
                spT_t[p] = GD("gpsimd", [lambda e, c=c: e.dma_start(
                    out=xactT_sp[c],
                    in_=bass.AP(tensor=sb_xT, offset=c * 128,
                                ap=[[NKC * T, 128], [T, NKC], [1, 128]]))],
                    waits=(evT[p],), key="spT" + str(p))
                for wb in range((2 * NIB * c) // NT, (2 * NIB * (c + 1)) // NT):
                    q = wb % 2
                    wqb = sb_wq[:, DM * q:DM * (q + 1)]
                    s8w = 24 + 4 * q
                    w_src = wg_in if wb < NIB else wu_in
                    r_src = wb if wb < NIB else wb - NIB
                    wql[q] = GD("sync", [lambda e, w_src=w_src, r_src=r_src, wqb=wqb:
                                         e.dma_start(out=wqb,
                                                     in_=w_src[128 * r_src:128 * (r_src + 1), :])],
                                waits=(wqigp[q],) + (setup_all if wb < 2 else ()),
                                key="ld2" + str(q))
                    wqr1 = G("vector", lambda e, wqb=wqb, s8w=s8w: e.reduce_max(
                        sb_st8[:, s8w:s8w + 1], wqb, axis=AX.X,
                        apply_absolute_value=True), waits=(wql[q],))
                    wqr2 = G("vector", lambda e, s8w=s8w: e.reciprocal(
                        sb_st8[:, s8w + 1:s8w + 2], sb_st8[:, s8w:s8w + 1]), waits=(wqr1,))
                    wqr3 = G("gpsimd", lambda e, s8w=s8w: e.tensor_scalar_mul(
                        sb_st8[:, s8w + 2:s8w + 3], sb_st8[:, s8w + 1:s8w + 2], 127.0),
                        waits=(wqr2,))
                    wqr4 = G("gpsimd", lambda e, s8w=s8w: e.tensor_scalar_mul(
                        sb_st8[:, s8w + 3:s8w + 4], sb_st8[:, s8w:s8w + 1], 1.0 / 127.0),
                        waits=(wqr1,))
                    wqsh[q] = G("scalar", lambda e, wqb=wqb, s8w=s8w: e.activation(
                        wqb, wqb, AF.Identity, bias=col("c1285"),
                        scale=sb_st8[:, s8w + 2:s8w + 3]), waits=(wqr3, wqr4, wqigp[q]))
                    wqcs[q] = G("scalar", lambda e, wqb=wqb, q=q: e.activation(
                        sb_wint[:, DM * q:DM * (q + 1)], wqb, AF.Identity),
                        waits=(wqsh[q], wqstp[q]))
                    if wb > 0:
                        emit_wq_back(wb - 1)
                if c == NT - 1:
                    emit_wq_back(2 * NIB - 1)

            if stop_after == "A":
                z1 = G("vector", lambda e: e.memset(sb_scr[:, :], 0.0),
                       waits=(mx_t, evT[0], evT[1], spk[0], spk[1], spT_t[0], spT_t[1],
                              tanh_t[0], tanh_t[1], sq[0], sq[1]))
                toks = []
                for c in range(NT):
                    toks.append(GD("gpsimd", [lambda e, c=c: e.dma_start(
                        out=out_t[128 * c:128 * (c + 1), :], in_=sb_scr[:, :])],
                        waits=(z1,), key="sto0"))
                G("gpsimd", lambda e: e.nop(), waits=(toks[-1],))
                with nc.Block() as block:
                    sc.finalize(block)
                return nc

            gm1 = G("tensor", lambda e: e.matmul(ps_a[0:1, 0:128], sb_max[:, 0:1],
                                                 sb_i128f[:, :], start=True, stop=True),
                    waits=(mx_t, evT[0], evT[1]))
            gm2 = G("vector", lambda e: e.tensor_copy(sb_scr[0:1, 0:128], ps_a[0:1, 0:128]),
                    waits=(gm1,))
            gm3 = G("vector", lambda e: e.reduce_max(tiny("gmax"), sb_scr[0:1, 0:128], axis=AX.X),
                    waits=(gm2,))
            gm4 = G("vector", lambda e: e.memset(sb_tiny[0:1, TI["pad"]:TI["pad"] + 16], 0.0),
                    waits=(gm3,))
            gm5 = G("vector", lambda e: e.tensor_copy(sb_tiny[0:1, TI["pad"]:TI["pad"] + 1],
                                                      tiny("gmax")), waits=(gm4,))
            gm6 = GD("gpsimd", [lambda e: e.dma_start(out=cb1_i[0:16],
                                                      in_=sb_tiny[0:1, TI["pad"]:TI["pad"] + 16])],
                     waits=(gm5,), key="dma_g")
            coll1_tok = G("gpsimd", lambda e: e.collective_compute(
                "AllReduce", ALU.max, replica_groups=[list(range(n_cores))],
                ins=[cb1_i.tensor.ap().opt()], outs=[cb1_o.tensor.ap().opt()]),
                waits=(gm6,), key="cc1")

            wld = [None, None]
            wcast = [None, None]
            wtp = [None, None]
            wev = [None, None]
            for r in range(NKC):
                p = r % 2
                dst = sb_scr if p == 0 else sb_scr2
                wld[p] = GD("sync", [lambda e, r=r, dst=dst: e.dma_start(
                    out=dst[:, :], in_=W_in[128 * r:128 * (r + 1), :])],
                    waits=(wcast[p],) + ((gm3, setup_all[1]) if r < 2 else ()), key="wW" + str(p))
                wcast[p] = G("vector", lambda e, dst=dst, p=p: e.tensor_copy(
                    sb_wtb[:, DM * p:DM * (p + 1)], dst[:, :]), waits=(wld[p], wtp[p]))
                def wtmm(e, p=p):
                    rr = None
                    for fc in range(NKC):
                        rr = e.matmul(ps_b[:, 1024 * p + 128 * fc:1024 * p + 128 * (fc + 1)],
                                      sb_wtb[:, DM * p + 128 * fc:DM * p + 128 * (fc + 1)],
                                      sb_i128b[:, :], start=True, stop=True)
                    return rr
                wtp[p] = G("tensor", wtmm, waits=(wcast[p], wev[p]))
                wev[p] = G("vector", lambda e, r=r, p=p: e.tensor_copy(
                    bass.AP(tensor=sb_WT, offset=r * 128,
                            ap=[[NKC * DM, 128], [DM, NKC], [1, 128]]),
                    ps_b[:, 1024 * p:1024 * p + 1024]), waits=(wtp[p],))
            wprep_done = (wev[0], wev[1])

            if stop_after == "W":
                z1 = G("vector", lambda e: e.memset(sb_scr[:, :], 0.0),
                       waits=wprep_done + (coll1_tok, mx_t, evT[0], evT[1], spk[0], spk[1],
                                           spT_t[0], spT_t[1], sq[0], sq[1]))
                toks = []
                for c in range(NT):
                    toks.append(GD("gpsimd", [lambda e, c=c: e.dma_start(
                        out=out_t[128 * c:128 * (c + 1), :], in_=sb_scr[:, :])],
                        waits=(z1,), key="sto0"))
                G("gpsimd", lambda e: e.nop(), waits=(toks[-1],))
                with nc.Block() as block:
                    sc.finalize(block)
                return nc

            sb_Y = alloc(phAB, "Ytile", [128, NH * 128], BF16)
            sb_X0 = alloc(phAB, "X0", [128, NH * 128], BF16)
            sb_X1 = alloc(phAB, "X1", [128, NH * 128], BF16)
            sb_dmt = alloc(phAB, "dmt", [128, NH * 128], BF16)
            sb_rt = alloc(phAB, "rtT", [64, NH * 128], BF16)
            sb_dk = alloc(phAB, "dkt", [128, DM], BF16)
            sb_u0 = alloc(phAB, "u0t", [128, NH * DH], BF16)
            sb_stat = alloc(phAB, "stattile", [128, DM], BF16)
            sb_xk = alloc(phAB, "xkstream", [128, 2 * DM], BF16)

            s0_ev = G("vector", lambda e: e.memset(sb_S0[:, :], 0.0), waits=setup_all)
            ldk = [None, None]
            ldk[0] = GD("sync", [lambda e: e.dma_start(out=sb_xk[:, 0:DM], in_=xacttok_sp[0])],
                        waits=(spk[0],), key="ldk0")
            spst_t = sppr_t = sprt_t = None
            st_ev = gr_ev = rt_ev = u0_ev = dk_t = yv_t = yk_t = None
            dbg_st_tok = None
            x_ev = [None, None, None]
            nt_lim = NT
            bsub = 99
            if stop_after and stop_after.startswith("B") and len(stop_after) > 1:
                parts = stop_after[1:].split(".")
                nt_lim = int(parts[0])
                if len(parts) > 1:
                    bsub = int(parts[1])
            for c in range(nt_lim):
                p = c % 2
                xkc = sb_xk[:, DM * p:DM * (p + 1)]
                if c + 1 < nt_lim:
                    q = (c + 1) % 2
                    ldk[q] = GD("sync", [lambda e, c=c, q=q: e.dma_start(
                        out=sb_xk[:, DM * q:DM * (q + 1)], in_=xacttok_sp[c + 1])],
                        waits=(spk[(c + 1) % 2], yk_t, dk_t), key="ldk" + str(q))
                def smms(e, c=c):
                    r = None
                    for half in range(2):
                        for fc in range(NKC):
                            r = e.matmul(ps_b[:, 512 * half:512 * (half + 1)],
                                         sb_xT[:, fc * T + c * 128: fc * T + (c + 1) * 128],
                                         sb_WT[:, fc * DM + 512 * half: fc * DM + 512 * (half + 1)],
                                         start=(fc == 0), stop=(fc == NKC - 1))
                    return r
                st_mm = G("tensor", smms,
                          waits=wprep_done + (evT[p], x_ev[2], u0_ev, st_ev))
                st_ev = G("vector", lambda e: e.tensor_copy(sb_stat[:, :], ps_b[:, 0:1024]),
                          waits=(st_mm, spst_t, yv_t, dbg_st_tok))
                spst_t = GD("gpsimd", [lambda e, c=c: e.dma_start(out=static_sp[c],
                                                                  in_=sb_stat[:, :])],
                            waits=(st_ev,), key="spst")
                if debug_outs and c == 0:
                    dbg_st_tok = GD("gpsimd", [lambda e: e.dma_start(out=dbg["d_static"], in_=sb_stat[:, :])],
                                    waits=(st_ev,), key="dbgs")
                if bsub <= 1:
                    break
                def gmms(e, c=c):
                    r = None
                    for h in [hh for hh in range(NH) if hh % 2 == 0] + \
                             [hh for hh in range(NH) if hh % 2 == 1]:
                        fc, po = h // 2, 64 * (h % 2)
                        sl = sb_xT[po:po + 64, fc * T + c * 128: fc * T + (c + 1) * 128]
                        r = e.matmul(ps_a[:, 128 * h:128 * (h + 1)], sl, sl,
                                     start=True, stop=True, tile_position=(po, 0))
                    return r
                gr_mm = G("tensor", gmms, waits=(evT[p], gr_ev, x_ev[1], rt_ev, s0_ev))
                gr_ev = G("vector", lambda e: e.tensor_tensor(sb_dmt[:, :], ps_a[:, 0:2048],
                                                              sb_mD[:, :], ALU.mult),
                          waits=(gr_mm,))
                if bsub <= 2:
                    break
                yv_t = G("vector", lambda e: e.tensor_scalar(
                    v3(sb_Y, 0, NH * 128, NH, 128, 64),
                    v3(sb_stat, 0, DM, NH, 64, 64),
                    col("beta"), None, ALU.mult), waits=(st_ev, x_ev[0], yk_t))
                yk_t = G("vector", lambda e, p=p: e.tensor_scalar(
                    v3(sb_Y, 64, NH * 128, NH, 128, 64),
                    bass.AP(tensor=sb_xk, offset=DM * p,
                            ap=[[2 * DM, 128], [64, NH], [1, 64]]),
                    col("blam_t"), None, ALU.mult), waits=(ldk[p], yv_t))
                if bsub <= 3:
                    break
                xprev = sb_Y
                prevtok = (yk_t,)
                for j in range(NEU):
                    pst = ps_b if (j % 2 == 0) else ps_a
                    xnew = (sb_X0, sb_X1, sb_X0)[j]
                    def nmms(e, pst=pst, xprev=xprev):
                        r = None
                        for h in range(NH):
                            r = e.matmul(pst[:, 128 * h:128 * (h + 1)],
                                         sb_dmt[:, 128 * h:128 * (h + 1)],
                                         xprev[:, 128 * h:128 * (h + 1)],
                                         start=True, stop=True)
                        return r
                    wx = (gr_ev,) + prevtok
                    if j == 0:
                        wx = wx + (st_ev, sppr_t, u0_ev)
                    elif j == 2:
                        wx = wx + (x_ev[0], sppr_t, u0_ev)
                    nm = G("tensor", nmms, waits=wx)
                    x_ev[j] = G("vector", lambda e, pst=pst, xnew=xnew: e.tensor_tensor(
                        xnew[:, :], pst[:, 0:2048], sb_Y[:, :], ALU.add), waits=(nm,))
                    xprev = xnew
                    prevtok = (x_ev[j],)
                X4 = xprev
                sppr_t = GD("gpsimd", [lambda e, c=c, X4=X4: e.dma_start(
                    out=pr_sp[c], in_=v3(X4, 0, NH * 128, NH, 128, 64))],
                            waits=(x_ev[NEU - 1],), key="sppr")
                if bsub <= 4:
                    break
                def rmms(e, X4=X4):
                    r = None
                    for h in range(NH):
                        r = e.matmul(ps_a[0:64, 128 * h:128 * (h + 1)],
                                     X4[:, 128 * h + 64:128 * (h + 1)],
                                     sb_i128b[:, :], start=True, stop=True)
                    return r
                rt_mm = G("tensor", rmms, waits=(x_ev[NEU - 1], gr_ev))
                rt_ev = G("vector", lambda e: e.tensor_copy(sb_rt[:, :], ps_a[0:64, 0:2048]),
                          waits=(rt_mm, sprt_t, u0_ev))
                sprt_t = GD("gpsimd", [lambda e, c=c: e.dma_start(out=rt_sp[c], in_=sb_rt[:, :])],
                            waits=(rt_ev,), key="sprt")
                if bsub <= 5:
                    break
                dk_t = G("scalar", lambda e, xkc=xkc: e.activation(
                    sb_dk[:, :], xkc, AF.Identity, scale=col("lam_ct")),
                    waits=(ldk[p], s0_ev, dk_t))
                def u0mms(e):
                    r = None
                    for h in range(NH):
                        r = e.matmul(ps_b[:, 64 * h:64 * (h + 1)],
                                     sb_rt[:, 128 * h:128 * (h + 1)],
                                     sb_S0[:, 64 * h:64 * (h + 1)],
                                     start=True, stop=True)
                    return r
                u0_mm = G("tensor", u0mms, waits=(rt_ev, s0_ev, x_ev[0]))
                u0_ev = G("vector", lambda e, X4=X4: e.scalar_tensor_tensor(
                    v3(sb_u0, 0, NH * DH, NH, 64, 64),
                    v3(ps_b, 0, 2048, NH, 64, 64), -1.0,
                    v3(X4, 0, NH * 128, NH, 128, 64),
                    ALU.mult, ALU.add), waits=(u0_mm,))
                if bsub <= 6:
                    break
                def s0mms(e):
                    r = None
                    for h in range(NH):
                        r = e.matmul(ps_a[0:64, 1024 + 64 * h:1024 + 64 * (h + 1)],
                                     sb_dk[:, 64 * h:64 * (h + 1)],
                                     sb_u0[:, 64 * h:64 * (h + 1)],
                                     start=True, stop=True)
                    return r
                s0_mm = G("tensor", s0mms, waits=(dk_t, u0_ev, gr_ev))
                s0_ev = G("vector", lambda e: e.scalar_tensor_tensor(
                    sb_S0[:, :], sb_S0[:, :], sb_cols[0:64, COL["lamC"]:COL["lamC"] + 1],
                    ps_a[0:64, 1024:2048], ALU.mult, ALU.add), waits=(s0_mm,))
            cb3a = G("vector", lambda e: e.tensor_copy(sb_scr[0:64, 0:1024], sb_S0[:, :]),
                     waits=(s0_ev, gr_ev, st_ev, yk_t, dk_t, u0_ev, rt_ev))
            cb3b = GD("gpsimd", [lambda e: e.dma_start(
                out=bass.AP(tensor=cb3_i.tensor, offset=0, ap=[[1024, 64], [1, 1024]]),
                in_=sb_scr[0:64, 0:1024])], waits=(cb3a,), key="dma_g")
            coll3_tok = G("gpsimd", lambda e: e.collective_compute(
                "AllGather", ALU.bypass, replica_groups=[list(range(n_cores))],
                ins=[cb3_i.tensor.ap().opt()], outs=[cb3_o.tensor.ap().opt()]),
                waits=(cb3b,), key="cc3")
            phB_done = sc.barrier()

        if stop_after is not None and stop_after.startswith("B"):
            z1 = G("vector", lambda e: e.memset(sb_scr[:, :], 0.0), waits=phB_done)
            toks = []
            for c in range(NT):
                toks.append(GD("gpsimd", [lambda e, c=c: e.dma_start(
                    out=out_t[128 * c:128 * (c + 1), :], in_=sb_scr[:, :])],
                    waits=(z1,), key="sto0"))
            G("gpsimd", lambda e: e.nop(), waits=(toks[-1],))
            with nc.Block() as block:
                sc.finalize(block)
            return nc

        with ExitStack() as phC:
            sb_xlT = alloc(phC, "xlevT", [128, NKC * T], BF16)
            sb_wgT = alloc(phC, "wgT", [128, NKC * IFF], BF16)
            sb_wuT = alloc(phC, "wuT", [128, NKC * IFF], BF16)
            sb_xbuf2 = alloc(phC, "xbuf2", [128, 2 * DM], F32)
            sb_u2 = alloc(phC, "ushift", [128, 2 * DM], F32)
            sb_int2 = alloc(phC, "int2", [128, DM], I32)
            sb_lev2 = alloc(phC, "lev2", [128, 2 * DM], BF16)
            sb_hbuf = alloc(phC, "hbuf", [128, 2 * 512], BF16)
            sb_silu = alloc(phC, "silu", [128, 512], F32)
            sb_hmax = alloc(phC, "hmax", [128, 2], F32)
            sb_st8c = alloc(phC, "statscrC", [128, 16], F32)
            sb_junkC = alloc(phC, "junkC", [128, 1024], BF16)

            pb = phB_done
            c1a = GD("sync", [lambda e: e.dma_start(out=sb_tiny[0:1, TI["pad"]:TI["pad"] + 16],
                                                    in_=cb1_o[0:16])],
                     waits=pb + (coll1_tok,), key="dma_s")
            c1b = G("vector", lambda e: e.tensor_copy(tiny("gmax"),
                                                      sb_tiny[0:1, TI["pad"]:TI["pad"] + 1]),
                    waits=(c1a,))
            if debug_outs:
                GD("gpsimd", [lambda e: e.dma_start(out=dbg["d_gmax"], in_=tiny("gmax"))],
                   waits=(c1b,), key="dbgs")
            c1c = G("vector", lambda e: e.tensor_tensor(tiny("t1"), tiny("sigcx"), tiny("gmax"),
                                                        ALU.mult), waits=(c1b,))
            c1d = G("vector", lambda e: e.tensor_scalar_mul(tiny("scalex"), tiny("t1"),
                                                            1.0 / 127.0), waits=(c1c,))
            c1e = G("vector", lambda e: e.reciprocal(tiny("t2"), tiny("t1")), waits=(c1d,))
            c1f = G("vector", lambda e: e.tensor_scalar_mul(tiny("invx"), tiny("t2"), 127.0),
                    waits=(c1e,))
            r1m = G("tensor", lambda e: e.matmul(ps_a[:, 0:1], sb_ones[0:1, :], tiny("invx"),
                                                 start=True, stop=True), waits=(c1f,) + pb)
            rx1 = G("vector", lambda e: e.tensor_copy(col("invx"), ps_a[:, 0:1]), waits=(r1m,))
            r2m = G("tensor", lambda e: e.matmul(ps_a[:, 0:1], sb_ones[0:1, :], tiny("scalex"),
                                                 start=True, stop=True), waits=(rx1,))
            rx2 = G("vector", lambda e: e.tensor_copy(col("scalex"), ps_a[:, 0:1]), waits=(r2m,))

            wl = [None, None]
            wsh = [None, None]
            wcl = [None, None]
            wcs = [None, None]
            wig = [None, None]
            wst = [None, None]
            wtp2 = [None, None]
            wev2 = [None, None]
            wred = [None, None]

            def quant_tile(p, src_tok, scale_col, bias_col, dst_emit, first_waits=()):
                ub = sb_u2[:, DM * p:DM * (p + 1)]
                ib2 = sb_int2[:, :]
                cb = sb_junkC[:, :]
                lb = sb_lev2[:, DM * p:DM * (p + 1)]
                xb2 = sb_xbuf2[:, DM * p:DM * (p + 1)]
                wsh[p] = G("scalar", lambda e: e.activation(
                    ub, xb2, AF.Identity, bias=bias_col, scale=scale_col),
                    waits=(src_tok, wcl[p], wig[p]) + first_waits)
                wcl[p] = G("vector", lambda e: e.tensor_scalar(
                    ub, ub, 255.99, 0.01, ALU.min, ALU.max), waits=(wsh[p], wcs[p]))
                wcs[p] = G("scalar", lambda e: e.activation(ib2, ub, AF.Identity),
                           waits=(wcl[p], wst[0], wst[1]))
                wig[p] = G("vector", lambda e: e.tensor_tensor(cb, ib2, ub, ALU.is_gt),
                           waits=(wcs[p], wst[0], wst[1]))
                wst[p] = G("vector", lambda e: e.scalar_tensor_tensor(
                    lb, ib2, 128.0, cb, ALU.subtract, ALU.subtract),
                    waits=(wig[p], wtp2[p]))
                def wtmm2(e, p=p, lb=lb):
                    rr = None
                    for fc in range(NKC):
                        rr = e.matmul(ps_b[:, 1024 * p + 128 * fc:1024 * p + 128 * (fc + 1)],
                                      lb[:, 128 * fc:128 * (fc + 1)],
                                      sb_i128b[:, :], start=True, stop=True)
                    return rr
                wtp2[p] = G("tensor", wtmm2, waits=(wst[p], wev2[p]) + first_waits)
                wev2[p] = G("vector", dst_emit, waits=(wtp2[p],))
                return wev2[p]

            for r in range(2 * NIB):
                p = r % 2
                lb = sb_lev2[:, DM * p:DM * (p + 1)]
                dst_T = sb_wgT if r < NIB else sb_wuT
                rr = r if r < NIB else r - NIB
                wl[p] = GD("sync", [lambda e, r=r, lb=lb: e.dma_start(
                    out=lb, in_=wlev_sp[r])],
                    waits=(wtp2[p],) + (pb if r < 2 else ()), key="ld2" + str(p))
                def wtmm2w(e, p=p, lb=lb):
                    rr2 = None
                    for fc in range(NKC):
                        rr2 = e.matmul(ps_b[:, 1024 * p + 128 * fc:1024 * p + 128 * (fc + 1)],
                                       lb[:, 128 * fc:128 * (fc + 1)],
                                       sb_i128b[:, :], start=True, stop=True)
                    return rr2
                wtp2[p] = G("tensor", wtmm2w, waits=(wl[p], wev2[p]) + (pb if r < 2 else ()))
                def dste(e, rr=rr, p=p, dst_T=dst_T):
                    return e.tensor_copy(
                        bass.AP(tensor=dst_T, offset=rr * 128,
                                ap=[[NKC * IFF, 128], [IFF, NKC], [1, 128]]),
                        ps_b[:, 1024 * p:1024 * p + 1024])
                wev2[p] = G("vector", dste, waits=(wtp2[p],))
            wgu_done = (wev2[0], wev2[1])

            sc_t = [None, None]
            for c in range(NT):
                p = c % 2
                xb2 = sb_xbuf2[:, DM * p:DM * (p + 1)]
                xlt = GD("sync", [lambda e, c=c, xb2=xb2: e.dma_start(
                    out=xb2, in_=x_in[128 * c:128 * (c + 1), :])],
                    waits=(wsh[p], wred[p]) if c < 2 else (wsh[p],), key="ld2" + str(p))
                s1 = G("gpsimd", lambda e, c=c, p=p: e.tensor_tensor(
                    sb_st8c[:, 8 * p + 3:8 * p + 4], sb_stats[:, 2 * c + 1:2 * c + 2],
                    col("invx"), ALU.mult), waits=(rx1, sc_t[p], wsh[p]))
                s2 = G("gpsimd", lambda e, c=c, p=p: e.tensor_tensor(
                    sb_st8c[:, 8 * p + 4:8 * p + 5], sb_stats[:, 2 * c:2 * c + 1],
                    sb_st8c[:, 8 * p + 3:8 * p + 4], ALU.mult), waits=(s1,))
                sc_t[p] = G("gpsimd", lambda e, p=p: e.tensor_scalar(
                    sb_st8c[:, 8 * p + 4:8 * p + 5], sb_st8c[:, 8 * p + 4:8 * p + 5],
                    -1.0, 128.5, ALU.mult, ALU.add), waits=(s2,))
                def dste(e, c=c, p=p):
                    return e.tensor_copy(
                        bass.AP(tensor=sb_xlT, offset=c * 128,
                                ap=[[NKC * T, 128], [T, NKC], [1, 128]]),
                        ps_b[:, 1024 * p:1024 * p + 1024])
                last = quant_tile(p, xlt, sb_st8c[:, 8 * p + 3:8 * p + 4],
                                  sb_st8c[:, 8 * p + 4:8 * p + 5], dste,
                                  first_waits=(sc_t[p],))
                if debug_outs and c == 0:
                    pass
            xlev_done = (wev2[0], wev2[1])

            hm0 = G("vector", lambda e: e.memset(sb_hmax[:, :], 0.0), waits=pb)
            NTT = T // 512
            regions = [ps_a[:, 0:1024], ps_a[:, 1024:2048], ps_b[:, 0:1024], ps_b[:, 1024:2048]]
            silu_t = [None] * 4
            hmul_t = [None] * 4
            hmul_last = None
            red_t = [None, None]
            hst_t = [None, None]
            mxu_t = hm0
            bidx = 0
            for tt_ in range(NTT):
                for ib in range(NIB):
                    q = bidx % 4
                    hp = bidx % 2
                    reg = regions[q]
                    def gmms2(e, ib=ib, tt_=tt_, reg=reg):
                        r = None
                        for which in range(2):
                            wT = sb_wgT if which == 0 else sb_wuT
                            for fc in range(NKC):
                                r = e.matmul(reg[:, 512 * which:512 * (which + 1)],
                                             wT[:, fc * IFF + 128 * ib: fc * IFF + 128 * (ib + 1)],
                                             sb_xlT[:, fc * T + 512 * tt_: fc * T + 512 * (tt_ + 1)],
                                             start=(fc == 0), stop=(fc == NKC - 1))
                        return r
                    base_w = (silu_t[q], hmul_t[q])
                    if bidx < 4:
                        base_w = base_w + wgu_done + xlev_done + (wtp2[0], wtp2[1])
                    gt = G("tensor", gmms2, waits=base_w)
                    silu_t[q] = G("scalar", lambda e, reg=reg: e.activation(
                        sb_silu[:, :], reg[:, 0:512],
                        AF.Silu, scale=col("scalex")),
                        waits=(gt, rx2, hmul_last))
                    hmul_t[q] = G("vector", lambda e, reg=reg, hp=hp: e.scalar_tensor_tensor(
                        sb_hbuf[:, 512 * hp:512 * (hp + 1)], reg[:, 512:1024],
                        col("scalex"),
                        sb_silu[:, :], ALU.mult, ALU.mult),
                        waits=(silu_t[q], rx2, hst_t[hp], red_t[hp]))
                    hmul_last = hmul_t[q]
                    red_t[hp] = G("vector", lambda e, hp=hp: e.reduce_max(
                        sb_hmax[:, 1:2], sb_hbuf[:, 512 * hp:512 * (hp + 1)], axis=AX.X,
                        apply_absolute_value=True), waits=(hmul_t[q], mxu_t))
                    mxu_t = G("gpsimd", lambda e: e.tensor_scalar(
                        sb_hmax[:, 0:1], sb_hmax[:, 1:2], sb_hmax[:, 0:1], None, ALU.max),
                        waits=(red_t[hp], mxu_t))
                    def hsp(e, ib=ib, tt_=tt_, hp=hp):
                        return e.dma_start(
                            out=bass.AP(tensor=h_sp.tensor, offset=(ib * 128) * T + tt_ * 512,
                                        ap=[[T, 128], [1, 512]]),
                            in_=sb_hbuf[:, 512 * hp:512 * (hp + 1)])
                    hst_t[hp] = GD("sync", [hsp], waits=(hmul_t[q],), key="hst" + str(hp))
                    bidx += 1

            h1 = G("tensor", lambda e: e.matmul(ps_a[0:1, 0:128], sb_hmax[:, 0:1],
                                                sb_i128f[:, :], start=True, stop=True),
                   waits=(mxu_t,) + tuple(silu_t) + tuple(hmul_t))
            h2 = G("vector", lambda e: e.tensor_copy(sb_scr[0:1, 0:128], ps_a[0:1, 0:128]),
                   waits=(h1,))
            h3 = G("vector", lambda e: e.reduce_max(tiny("hmax"), sb_scr[0:1, 0:128], axis=AX.X),
                   waits=(h2,))
            h4 = G("vector", lambda e: e.memset(sb_tiny[0:1, TI["pad"]:TI["pad"] + 16], 0.0),
                   waits=(h3,))
            h5 = G("vector", lambda e: e.tensor_copy(sb_tiny[0:1, TI["pad"]:TI["pad"] + 1],
                                                     tiny("hmax")), waits=(h4,))
            h6 = GD("gpsimd", [lambda e: e.dma_start(out=cb2_i[0:16],
                                                     in_=sb_tiny[0:1, TI["pad"]:TI["pad"] + 16])],
                    waits=(h5, hst_t[0], hst_t[1]), key="dma_g")
            coll2_tok = G("gpsimd", lambda e: e.collective_compute(
                "AllReduce", ALU.max, replica_groups=[list(range(n_cores))],
                ins=[cb2_i.tensor.ap().opt()], outs=[cb2_o.tensor.ap().opt()]),
                waits=(h6,), key="cc2")
            phC_done = sc.barrier()

        if stop_after == "C":
            z1 = G("vector", lambda e: e.memset(sb_scr[:, :], 0.0), waits=phC_done + (coll1_tok, coll2_tok, coll3_tok))
            toks = []
            for c in range(NT):
                toks.append(GD("gpsimd", [lambda e, c=c: e.dma_start(
                    out=out_t[128 * c:128 * (c + 1), :], in_=sb_scr[:, :])],
                    waits=(z1,), key="sto0"))
            G("gpsimd", lambda e: e.nop(), waits=(toks[-1],))
            with nc.Block() as block:
                sc.finalize(block)
            return nc

        with ExitStack() as phDE:
            sb_attn = alloc(phDE, "attnring", [128, 6 * DM], BF16)
            sb_gb = alloc(phDE, "gammarep", [128, DM], F32)
            sb_lnbrow = alloc(phDE, "lnbrow", [1, DM], F32)
            sb_brow = alloc(phDE, "betarow", [1, DM], BF16)
            sb_S = alloc(phDE, "S_tile", [64, NH * DH], BF16)
            sb_xTc = alloc(phDE, "xTc", [64, 2 * NH * 128], BF16)
            sb_xkc2 = alloc(phDE, "xkc2", [128, 2 * DM], BF16)
            sb_stc = alloc(phDE, "stc", [128, 2 * DM], BF16)
            sb_prc = alloc(phDE, "prc", [128, 2 * NH * DH], BF16)
            sb_rtc = alloc(phDE, "rtc", [64, 2 * NH * 128], BF16)
            sb_aot = alloc(phDE, "aot", [128, NH * 128], BF16)
            sb_dk2 = alloc(phDE, "dk2", [128, DM], BF16)
            sb_U = alloc(phDE, "Utile", [128, NH * DH], BF16)
            sb_y = alloc(phDE, "ytile", [128, 2 * DM], F32)
            sb_junkDE = alloc(phDE, "junkDE", [128, 1024], BF16)
            sb_wdT = alloc(phDE, "wdT", [128, NIB * DM], BF16)
            sb_hc = alloc(phDE, "hc", [128, 2 * IFF], BF16)
            sb_hu = alloc(phDE, "hu", [128, 2 * IFF], F32)
            sb_hlev = alloc(phDE, "hlev", [128, 2 * IFF], BF16)
            sb_hint = alloc(phDE, "hint", [128, IFF], I32)
            sb_xr = alloc(phDE, "xr", [128, DM], F32)
            sb_outb = alloc(phDE, "outb", [128, DM], F32)
            sb_st8d = alloc(phDE, "st8d", [128, 16], F32)

            pc = phC_done
            g1 = GD("sync", [lambda e: e.dma_start(out=sb_scr[0:1, 0:1024], in_=lng_in)],
                    waits=pc, key="dma_s")
            def repmm_a(e):
                r = None
                for hh in range(2):
                    r = e.matmul(ps_a[:, 512 * hh:512 * (hh + 1)], sb_ones[0:1, :],
                                 sb_scr[0:1, 512 * hh:512 * (hh + 1)], start=True, stop=True)
                return r
            g2 = G("tensor", repmm_a, waits=(g1,) + pc)
            g3 = G("vector", lambda e: e.tensor_copy(sb_gb[:, :], ps_a[:, 0:1024]),
                   waits=(g2,) + pc)
            g4 = GD("sync", [lambda e: e.dma_start(out=sb_lnbrow[:, :], in_=lnb_in)],
                    waits=pc, key="dma_s")

            e7 = GD("sync", [lambda e: e.dma_start(out=sb_tiny[0:1, TI["pad"]:TI["pad"] + 16],
                                                   in_=cb2_o[0:16])],
                    waits=(coll2_tok,) + pc, key="dma_s")
            e8 = G("vector", lambda e: e.tensor_copy(tiny("hmax"),
                                                     sb_tiny[0:1, TI["pad"]:TI["pad"] + 1]),
                   waits=(e7,))
            if debug_outs:
                GD("gpsimd", [lambda e: e.dma_start(out=dbg["d_hmax"], in_=tiny("hmax"))],
                   waits=(e8,), key="dbgs")
            e9 = G("vector", lambda e: e.tensor_tensor(tiny("t1"), tiny("sigcd"),
                                                       tiny("hmax"), ALU.mult), waits=(e8,))
            e10 = G("vector", lambda e: e.tensor_scalar_mul(tiny("scaleh"), tiny("t1"),
                                                            1.0 / 127.0), waits=(e9,))
            e11 = G("vector", lambda e: e.reciprocal(tiny("t2"), tiny("t1")), waits=(e10,))
            e12 = G("vector", lambda e: e.tensor_scalar_mul(tiny("invh"), tiny("t2"), 127.0),
                    waits=(e11,))
            e13 = G("tensor", lambda e: e.matmul(ps_b[:, 0:1], sb_ones[0:1, :], tiny("invh"),
                                                 start=True, stop=True), waits=(e12, g2))
            e14 = G("vector", lambda e: e.tensor_copy(col("invh"), ps_b[:, 0:1]), waits=(e13,))
            e15 = G("tensor", lambda e: e.matmul(ps_b[:, 0:1], sb_ones[0:1, :], tiny("scaleh"),
                                                 start=True, stop=True), waits=(e14,))
            e16 = G("vector", lambda e: e.tensor_copy(col("scaleh"), ps_b[:, 0:1]),
                    waits=(e15,))
            eb1 = G("vector", lambda e: e.tensor_scalar(
                sb_scr[0:1, 0:DM], sb_lnbrow[:, :], tiny("invh"), None, ALU.mult),
                waits=(e12, g4))
            eb2 = G("vector", lambda e: e.tensor_copy(sb_brow[:, :], sb_scr[0:1, 0:DM]),
                    waits=(eb1,))

            d2 = G("vector", lambda e: e.memset(sb_scr[0:64, 0:1024], 0.0), waits=(eb2,))
            prev = d2
            dld = None
            for pair in range((n_cores + 1) // 2):
                nsl = min(2, n_cores - 2 * pair)
                dld = GD("sync", [lambda e, pair=pair, nsl=nsl: e.dma_start(
                    out=sb_y[0:64, 0:1024 * nsl],
                    in_=bass.AP(tensor=cb3_o.tensor, offset=2 * pair * 64 * NH * DH,
                                ap=[[1024, 64], [64 * NH * DH, nsl], [1, 1024]]))],
                    waits=(coll3_tok, prev) + (pc if pair == 0 else ()), key="dma_s")
                for k in range(nsl):
                    jc = 2 * pair + k
                    prev = G("vector", lambda e, jc=jc, k=k: e.scalar_tensor_tensor(
                        sb_scr[0:64, 0:1024], sb_y[0:64, 1024 * k:1024 * (k + 1)],
                        sb_sel[:, jc:jc + 1], sb_scr[0:64, 0:1024], ALU.mult, ALU.add),
                        waits=(prev, dld))
            s_ev = G("vector", lambda e: e.tensor_copy(sb_S[:, :], sb_scr[0:64, 0:1024]),
                     waits=(prev,))
            if debug_outs:
                GD("gpsimd", [lambda e: e.dma_start(out=dbg["d_sstart"],
                                                    in_=sb_scr[0:64, 0:1024])],
                   waits=(s_ev,), key="dbgs")

            ewl = [None, None]
            ewsh = [None, None]
            ewcl = [None, None]
            ewcs = ewig = None
            ewigp = [None, None]
            ewst = [None, None]
            ew_ev = None
            for r in range(DM // 128):
                p = r % 2
                hcb = sb_hu[:, IFF * p:IFF * (p + 1)]
                hlb = sb_hlev[:, IFF * p:IFF * (p + 1)]
                ewl[p] = GD("sync", [lambda e, r=r, hcb=hcb: e.dma_start(
                    out=hcb, in_=wd_in[128 * r:128 * (r + 1), :])],
                    waits=(ewigp[p],) + (pc if r < 2 else ()), key="eht" + str(p))
                ewr1 = G("vector", lambda e, hcb=hcb, p=p: e.reduce_max(
                    sb_st8d[:, 8 * p:8 * p + 1], hcb, axis=AX.X, apply_absolute_value=True),
                    waits=(ewl[p],))
                ewr2 = G("vector", lambda e, r=r, p=p: e.tensor_scalar_mul(
                    sb_wscale[:, 2 * NIB + r:2 * NIB + r + 1], sb_st8d[:, 8 * p:8 * p + 1],
                    1.0 / 127.0), waits=(ewr1,))
                ewr3 = G("vector", lambda e, p=p: e.reciprocal(
                    sb_st8d[:, 8 * p + 1:8 * p + 2], sb_st8d[:, 8 * p:8 * p + 1]),
                    waits=(ewr2,))
                ewr4 = G("vector", lambda e, p=p: e.tensor_scalar_mul(
                    sb_st8d[:, 8 * p + 2:8 * p + 3], sb_st8d[:, 8 * p + 1:8 * p + 2], 127.0),
                    waits=(ewr3,))
                ewsh[p] = G("scalar", lambda e, hcb=hcb, p=p: e.activation(
                    hcb, hcb, AF.Identity, bias=col("c1285"),
                    scale=sb_st8d[:, 8 * p + 2:8 * p + 3]),
                    waits=(ewr4, ewcl[p]))
                ewcs = G("scalar", lambda e, hcb=hcb: e.activation(
                    sb_hint[:, :], hcb, AF.Identity), waits=(ewsh[p], ewig))
                ewig = G("vector", lambda e, hcb=hcb, hlb=hlb: e.tensor_tensor(
                    hlb, sb_hint[:, :], hcb, ALU.is_gt), waits=(ewcs, ewst[p], ew_ev))
                ewigp[p] = ewig
                ews1 = G("vector", lambda e, hlb=hlb: e.scalar_tensor_tensor(
                    hlb, sb_hint[:, :], 128.0, hlb, ALU.subtract, ALU.subtract),
                    waits=(ewig,))
                ewst[p] = G("scalar", lambda e, hlb=hlb, r=r: e.activation(
                    hlb, hlb, AF.Identity,
                    scale=sb_wscale[:, 2 * NIB + r:2 * NIB + r + 1]),
                    waits=(ews1,))
                for sb8 in range(3):
                    n8 = min(8, NIB - 8 * sb8)
                    def ewmm8(e, hlb=hlb, sb8=sb8, n8=n8):
                        rr = None
                        for k8 in range(n8):
                            ib = 8 * sb8 + k8
                            rr = e.matmul(ps_b[:, 1024 + 128 * k8:1024 + 128 * (k8 + 1)],
                                          hlb[:, 128 * ib:128 * (ib + 1)],
                                          sb_i128b[:, :], start=True, stop=True)
                        return rr
                    mt8 = G("tensor", ewmm8, waits=(ewst[p], ew_ev) +
                            ((e15, g2) if r == 0 and sb8 == 0 else ()))
                    def ewev8(e, r=r, sb8=sb8, n8=n8):
                        return e.tensor_copy(
                            bass.AP(tensor=sb_wdT, offset=(8 * sb8) * DM + r * 128,
                                    ap=[[NIB * DM, 128], [DM, n8], [1, 128]]),
                            bass.AP(tensor=ps_b, offset=1024,
                                    ap=[[2048, 128], [128, n8], [1, 128]]))
                    ew_ev = G("vector", ewev8, waits=(mt8, e16))
            wd_done = (ew_ev, ewst[0], ewst[1], ewig)

            def issue_loads(c, buf, waits):
                k = "ldd" + str(buf)
                fns = []
                for par in range(2):
                    fns.append(lambda e, c=c, buf=buf, par=par: e.dma_start(
                        out=bass.AP(tensor=sb_xTc, offset=buf * NH * 128 + par * 128,
                                    ap=[[2 * NH * 128, 64], [256, NKC], [1, 128]]),
                        in_=bass.AP(tensor=xactT_sp.tensor,
                                    offset=c * 128 * NKC * 128 + par * 64 * NKC * 128,
                                    ap=[[NKC * 128, 64], [128, NKC], [1, 128]])))
                fns.append(lambda e, c=c, buf=buf: e.dma_start(
                    out=sb_xkc2[:, buf * DM:(buf + 1) * DM], in_=xacttok_sp[c]))
                fns.append(lambda e, c=c, buf=buf: e.dma_start(
                    out=sb_stc[:, buf * DM:(buf + 1) * DM], in_=static_sp[c]))
                fns.append(lambda e, c=c, buf=buf: e.dma_start(
                    out=sb_prc[:, buf * NH * DH:(buf + 1) * NH * DH], in_=pr_sp[c]))
                fns.append(lambda e, c=c, buf=buf: e.dma_start(
                    out=sb_rtc[:, buf * NH * 128:(buf + 1) * NH * 128], in_=rt_sp[c]))
                return GD("sync", fns, waits=waits, key=k)

            def hload(c, p, waits):
                def f(e, c=c, p=p):
                    return e.dma_start(
                        out=sb_hc[:, IFF * p:IFF * (p + 1)],
                        in_=bass.AP(tensor=h_sp.tensor, offset=c * 128,
                                    ap=[[T, 128], [128 * T, NIB], [1, 128]]))
                return GD("gpsimd", [f], waits=waits, key="hst" + str(p))

            ld5 = [None, None]
            ld5[0] = issue_loads(0, 0, pc)
            NTE = NT + 4
            ht_t = [None] * NT
            esh_t = [None] * NT
            ecl_t = [None] * NT
            ecs_t = [None] * NT
            eig_t = [None] * NT
            est_t = [None] * NT
            dmm_t = [None] * NT
            ff_t = [None] * NT
            o1_t = [None] * NT
            o2_t = [None] * NT
            ost_t = [None] * NT
            xt_t = [None] * NT
            attnw_t = [None] * NT
            ht_t[0] = hload(0, 0, pc)
            if NT > 1:
                ht_t[1] = hload(1, 1, pc)
            xt_t[0] = GD("sync", [lambda e: e.dma_start(out=sb_xr[:, :], in_=x_in[0:128, :])],
                         waits=pc, key="ext0")

            gr2a = gr2b = u_ev = ks_ev = None
            y_t = [None, None]
            su_ev = s_ev
            dk2_t = None
            attn_prev = None
            kst = smt = None
            tok = lambda a, j: (a[j] if 0 <= j < NT else None)
            for i in range(NTE):
                j = i - 4
                if 0 <= j < NT:
                    ff_t[j] = G("scalar", lambda e: e.activation(
                        sb_scr2[:, :], ps_b[:, 1024:2048], AF.Identity,
                        scale=col("scaleh")),
                        waits=(dmm_t[j], e16, tok(o2_t, j - 1)))
                    o1_t[j] = G("vector", lambda e, j=j: e.scalar_tensor_tensor(
                        sb_outb[:, :], sb_xr[:, :], 0.0,
                        sb_attn[:, DM * (j % 6):DM * (j % 6 + 1)], ALU.add, ALU.add),
                        waits=(xt_t[j], attnw_t[j], tok(ost_t, j - 1)))
                    if j + 1 < NT:
                        xt_t[j + 1] = GD("sync", [lambda e, j=j: e.dma_start(
                            out=sb_xr[:, :], in_=x_in[128 * (j + 1):128 * (j + 2), :])],
                            waits=(o1_t[j],), key="ext" + str(j % 2))
                    o2_t[j] = G("gpsimd", lambda e: e.tensor_tensor(
                        sb_outb[:, :], sb_outb[:, :], sb_scr2[:, :], ALU.add),
                        waits=(o1_t[j], ff_t[j]))
                    ost_t[j] = GD("gpsimd", [lambda e, j=j: e.dma_start(
                        out=out_t[128 * j:128 * (j + 1), :], in_=sb_outb[:, :])],
                        waits=(o2_t[j],), key="sto" + str(j % 2))
                j = i - 3
                if 0 <= j < NT:
                    p = j % 2
                    hlb = sb_hlev[:, IFF * p:IFF * (p + 1)]
                    def dmms(e, hlb=hlb):
                        r = None
                        for half in range(2):
                            for ib in range(NIB):
                                e.matmul(ps_b[:, 1024 + 512 * half:1024 + 512 * (half + 1)],
                                         hlb[:, 128 * ib:128 * (ib + 1)],
                                         sb_wdT[:, ib * DM + 512 * half:
                                                ib * DM + 512 * (half + 1)],
                                         start=(ib == 0), stop=False)
                            r = e.matmul(ps_b[:, 1024 + 512 * half:1024 + 512 * (half + 1)],
                                         sb_onesb[0:1, :],
                                         sb_brow[0:1, 512 * half:512 * (half + 1)],
                                         start=False, stop=True)
                        return r
                    dmm_t[j] = G("tensor", dmms, waits=(est_t[j], tok(ff_t, j - 1), eb2) +
                                 (wd_done + (e15,) if j < 2 else ()))
                j = i - 2
                if 0 <= j < NT:
                    p = j % 2
                    hub = sb_hu[:, IFF * p:IFF * (p + 1)]
                    hlb = sb_hlev[:, IFF * p:IFF * (p + 1)]
                    eig_t[j] = G("vector", lambda e, hub=hub, hlb=hlb: e.tensor_tensor(
                        hlb, sb_hint[:, :], hub, ALU.is_gt),
                        waits=(ecs_t[j], tok(est_t, j - 1), tok(dmm_t, j - 2)) +
                        (wd_done if j < 2 else ()))
                    est_t[j] = G("vector", lambda e, hlb=hlb: e.scalar_tensor_tensor(
                        hlb, sb_hint[:, :], 128.0, hlb, ALU.subtract, ALU.subtract),
                        waits=(eig_t[j],))
                j = i - 1
                if 0 <= j < NT:
                    p = j % 2
                    hub = sb_hu[:, IFF * p:IFF * (p + 1)]
                    ecl_t[j] = G("gpsimd", lambda e, hub=hub: e.tensor_scalar(
                        hub, hub, 255.99, 0.01, ALU.min, ALU.max),
                        waits=(esh_t[j], tok(ecs_t, j - 1)))
                    ecs_t[j] = G("scalar", lambda e, hub=hub: e.activation(
                        sb_hint[:, :], hub, AF.Identity),
                        waits=(ecl_t[j], tok(est_t, j - 1)) + (wd_done if j == 0 else ()))
                j = i
                if j < NT:
                    p = j % 2
                    hub = sb_hu[:, IFF * p:IFF * (p + 1)]
                    esh_t[j] = G("scalar", lambda e, p=p, hub=hub: e.activation(
                        hub, sb_hc[:, IFF * p:IFF * (p + 1)], AF.Identity,
                        bias=col("c1285"), scale=col("invh")),
                        waits=(ht_t[j], e14, tok(eig_t, j - 2)) + (wd_done if j < 2 else ()))
                    if j + 2 < NT:
                        ht_t[j + 2] = hload(j + 2, j % 2, (esh_t[j],))

                c = i
                if c >= NT:
                    continue
                buf = c % 2
                yb = sb_y[:, DM * buf:DM * (buf + 1)]
                if c + 1 < NT:
                    ld5[(c + 1) % 2] = issue_loads(
                        c + 1, (c + 1) % 2,
                        (gr2a, gr2b, u_ev, dk2_t, y_t[(c + 1) % 2])
                        if c >= 1 else (pc + (s_ev,)))
                xTc = sb_xTc[:, buf * NH * 128:(buf + 1) * NH * 128]
                xkc = sb_xkc2[:, buf * DM:(buf + 1) * DM]
                stc = sb_stc[:, buf * DM:(buf + 1) * DM]
                def gmm2(e, xTc=xTc):
                    r = None
                    for h in range(NH):
                        sl = xTc[:, 128 * h:128 * (h + 1)]
                        r = e.matmul(ps_a[:, 128 * h:128 * (h + 1)], sl, sl,
                                     start=True, stop=True)
                    return r
                grt = G("tensor", gmm2, waits=(ld5[buf], gr2a, gr2b, y_t[0], y_t[1], su_ev)
                        + ((g3,) if c == 0 else ()))
                gr2a = G("vector", lambda e: e.tensor_tensor(
                    sb_aot[:, :], ps_a[:, 0:2048], sb_mA[:, :], ALU.mult),
                    waits=(grt,))
                gr2b = gr2a
                def umms(e, buf=buf):
                    r = None
                    for h in range(NH):
                        r = e.matmul(ps_b[:, 64 * h:64 * (h + 1)],
                                     sb_rtc[:, buf * NH * 128 + 128 * h:
                                            buf * NH * 128 + 128 * (h + 1)],
                                     sb_S[:, 64 * h:64 * (h + 1)],
                                     start=True, stop=True)
                    return r
                umt = G("tensor", umms, waits=(ld5[buf], su_ev, u_ev, ks_ev) +
                        ((e14, e16) if c == 0 else ()))
                u_ev = G("vector", lambda e, buf=buf: e.scalar_tensor_tensor(
                    v3(sb_U, 0, NH * DH, NH, 64, 64),
                    v3(ps_b, 0, 2048, NH, 64, 64), -1.0,
                    bass.AP(tensor=sb_prc, offset=buf * NH * DH,
                            ap=[[2 * NH * DH, 128], [64, NH], [1, 64]]),
                    ALU.mult, ALU.add), waits=(umt,))
                def kmms(e, xTc=xTc):
                    r = None
                    for h in range(NH):
                        r = e.matmul(ps_b[:, 64 * h:64 * (h + 1)],
                                     xTc[:, 128 * h:128 * (h + 1)],
                                     sb_S[:, 64 * h:64 * (h + 1)],
                                     start=True, stop=True)
                    return r
                kst = G("tensor", kmms, waits=(u_ev,))
                ks_ev = G("vector", lambda e, stc=stc: e.scalar_tensor_tensor(
                    sb_ks[:, :], ps_b[:, 0:1024], col("lam_t"), stc, ALU.mult, ALU.add),
                    waits=(kst, ld5[buf]))
                def omms(e):
                    r = None
                    for h in range(NH):
                        r = e.matmul(ps_a[:, 64 * h:64 * (h + 1)],
                                     sb_aot[:, 128 * h:128 * (h + 1)],
                                     sb_U[:, 64 * h:64 * (h + 1)],
                                     start=True, stop=True)
                    return r
                omt = G("tensor", omms, waits=(u_ev, gr2a, gr2b))
                dk2_t = G("scalar", lambda e, xkc=xkc: e.activation(
                    sb_dk2[:, :], xkc, AF.Identity, scale=col("lam_ct")),
                    waits=(ld5[buf], dk2_t, smt))
                def smms(e):
                    r = None
                    for h in range(NH):
                        r = e.matmul(ps_a[0:64, 1024 + 64 * h:1024 + 64 * (h + 1)],
                                     sb_dk2[:, 64 * h:64 * (h + 1)],
                                     sb_U[:, 64 * h:64 * (h + 1)],
                                     start=True, stop=True)
                    return r
                smt = G("tensor", smms, waits=(dk2_t, u_ev))
                su_ev = G("vector", lambda e: e.scalar_tensor_tensor(
                    sb_S[:, :], sb_S[:, :], sb_cols[0:64, COL["lamC"]:COL["lamC"] + 1],
                    ps_a[0:64, 1024:2048], ALU.mult, ALU.add), waits=(smt,))
                y_t[buf] = G("vector", lambda e, yb=yb: e.tensor_tensor(
                    yb, ps_a[:, 0:1024], sb_ks[:, :], ALU.add), waits=(omt, ks_ev))
                if debug_outs and c == 0:
                    y_t[buf] = GD("gpsimd", [lambda e, yb=yb: e.dma_start(out=dbg["d_y"],
                                                                          in_=yb)],
                                  waits=(y_t[buf],), key="dbgs")
                sc8 = 8 * buf
                l1 = G("scalar", lambda e, yb=yb, sc8=sc8: e.activation(
                    sb_junkDE[:, :], yb, AF.Square, accum_out=sb_stats[:, sc8:sc8 + 1]),
                    waits=(y_t[buf],))
                l2 = G("vector", lambda e, yb=yb, sc8=sc8: e.reduce_sum(
                    sb_stats[:, sc8 + 1:sc8 + 2], yb, axis=AX.X), waits=(y_t[buf],))
                l3 = G("gpsimd", lambda e, sc8=sc8: e.tensor_scalar_mul(
                    sb_stats[:, sc8 + 2:sc8 + 3], sb_stats[:, sc8 + 1:sc8 + 2], 1.0 / DM),
                    waits=(l2,))
                l4 = G("gpsimd", lambda e, sc8=sc8: e.tensor_tensor(
                    sb_stats[:, sc8 + 3:sc8 + 4], sb_stats[:, sc8 + 2:sc8 + 3],
                    sb_stats[:, sc8 + 2:sc8 + 3], ALU.mult), waits=(l3,))
                l5 = G("vector", lambda e, sc8=sc8: e.scalar_tensor_tensor(
                    sb_stats[:, sc8 + 4:sc8 + 5], sb_stats[:, sc8 + 0:sc8 + 1],
                    1.0 / DM, sb_stats[:, sc8 + 3:sc8 + 4], ALU.mult, ALU.subtract),
                    waits=(l4, l1))
                l6 = G("gpsimd", lambda e, sc8=sc8: e.tensor_scalar_add(
                    sb_stats[:, sc8 + 4:sc8 + 5], sb_stats[:, sc8 + 4:sc8 + 5], 1e-5),
                    waits=(l5,))
                l7 = G("scalar", lambda e, sc8=sc8: e.activation(
                    sb_stats[:, sc8 + 5:sc8 + 6], sb_stats[:, sc8 + 4:sc8 + 5], AF.Sqrt),
                    waits=(l6,))
                l8 = G("vector", lambda e, sc8=sc8: e.reciprocal(
                    sb_stats[:, sc8 + 6:sc8 + 7], sb_stats[:, sc8 + 5:sc8 + 6]), waits=(l7,))
                l9 = G("gpsimd", lambda e, sc8=sc8: e.tensor_tensor(
                    sb_stats[:, sc8 + 7:sc8 + 8], sb_stats[:, sc8 + 6:sc8 + 7],
                    sb_stats[:, sc8 + 6:sc8 + 7], ALU.mult), waits=(l8,))
                l10 = G("gpsimd", lambda e, sc8=sc8: e.tensor_tensor(
                    sb_stats[:, sc8 + 7:sc8 + 8], sb_stats[:, sc8 + 7:sc8 + 8],
                    sb_stats[:, sc8 + 4:sc8 + 5], ALU.mult), waits=(l9,))
                l11 = G("gpsimd", lambda e, sc8=sc8: e.tensor_scalar(
                    sb_stats[:, sc8 + 7:sc8 + 8], sb_stats[:, sc8 + 7:sc8 + 8],
                    -0.5, 1.5, ALU.mult, ALU.add), waits=(l10,))
                l12 = G("gpsimd", lambda e, sc8=sc8: e.tensor_tensor(
                    sb_stats[:, sc8 + 6:sc8 + 7], sb_stats[:, sc8 + 6:sc8 + 7],
                    sb_stats[:, sc8 + 7:sc8 + 8], ALU.mult), waits=(l11,))
                lapply = G("vector", lambda e, yb=yb, sc8=sc8: e.tensor_scalar(
                    sb_scr[:, :], yb, sb_stats[:, sc8 + 2:sc8 + 3],
                    sb_stats[:, sc8 + 6:sc8 + 7], ALU.subtract, ALU.mult),
                    waits=(l12, attn_prev))
                attnw_t[c] = G("gpsimd", lambda e, c=c: e.tensor_tensor(
                    sb_attn[:, DM * (c % 6):DM * (c % 6 + 1)], sb_scr[:, :], sb_gb[:, :],
                    ALU.mult), waits=(lapply, tok(o1_t, c - 6)) + ((g3,) if c == 0 else ()))
                attn_prev = attnw_t[c]
            G("gpsimd", lambda e: e.nop(), waits=(ost_t[NT - 1],))

        with nc.Block() as block:
            sc.finalize(block)
    return nc


_NC_CACHE = {}

def _get_nc(T, n_cores, debug_outs=False):
    key = (T, n_cores, debug_outs)
    if key not in _NC_CACHE:
        _NC_CACHE[key] = build_nc(T, n_cores, debug_outs)
    return _NC_CACHE[key]


def make_host_constants():
    j = np.arange(128)
    tm, sm = np.meshgrid(j, j, indexing="ij")
    diffT = (tm - sm).T
    pos = np.where(diffT > 0, diffT, 0)
    ind = np.zeros((7, 128, 128), dtype=np.float32)
    for b in range(7):
        ind[b] = ((pos >> b) & 1).astype(np.float32)
    return {
        "ind_bits": ind,
        "strictT": (diffT > 0).astype(np.float32),
        "inclT": (diffT >= 0).astype(np.float32),
        "i128_bf16": np.eye(128, dtype=BFNP),
        "i128_f32": np.eye(128, dtype=np.float32),
        "i64_f32": np.eye(64, dtype=np.float32),
        "ones_row": np.ones((1, 128), dtype=np.float32),
    }


def make_in_maps(inputs, n_cores=8):
    x = np.ascontiguousarray(np.asarray(inputs["L0_attn_x"], dtype=np.float32))
    B, L, Dm = x.shape
    Tall = B * L
    Tc = Tall // n_cores
    cores_per_batch = max(1, n_cores // B)
    xf = x.reshape(Tall, Dm)
    consts = make_host_constants()
    common = dict(
        W=np.ascontiguousarray(inputs["L0_attn_W"], dtype=np.float32),
        wg=np.ascontiguousarray(inputs["L0_ffn_w_gate"], dtype=np.float32),
        wu=np.ascontiguousarray(inputs["L0_ffn_w_up"], dtype=np.float32),
        wd=np.ascontiguousarray(inputs["L0_ffn_w_down"], dtype=np.float32),
        ln_g=np.ascontiguousarray(np.asarray(inputs["L0_attn_ln_gamma"],
                                             dtype=np.float32).reshape(1, Dm)),
        ln_b=np.ascontiguousarray(np.asarray(inputs["L0_attn_ln_beta"],
                                             dtype=np.float32).reshape(1, Dm)),
        eta=np.asarray(inputs["L0_attn_eta"], dtype=np.float32).reshape(1, 1),
        lam_logit=np.asarray(inputs["L0_attn_lam_logit"], dtype=np.float32).reshape(1, 1),
        clip_x=np.asarray(inputs["L0_ffn_clip_x"], dtype=np.float32).reshape(1, 1),
        clip_down=np.asarray(inputs["L0_ffn_clip_down"], dtype=np.float32).reshape(1, 1),
        **consts,
    )
    in_maps = []
    for i in range(n_cores):
        sel = np.zeros((64, n_cores), dtype=np.float32)
        if i % cores_per_batch != 0:
            sel[:, i - 1] = 1.0
        m = dict(common)
        m["x"] = np.ascontiguousarray(xf[i * Tc:(i + 1) * Tc])
        m["sel"] = sel
        in_maps.append(m)
    return in_maps, Tc


def kernel(**inputs):
    x = np.asarray(inputs["L0_attn_x"])
    B, L, Dm = x.shape
    n_cores = 8
    in_maps, Tc = make_in_maps(inputs, n_cores)
    nc = _get_nc(Tc, n_cores)
    res = run_bass_kernel_spmd(nc, in_maps, core_ids=list(range(n_cores)))
    out = np.concatenate([np.asarray(r["out"]) for r in res.results], axis=0)
    return out.reshape(B, L, Dm).astype(np.float32)
